# version 01
# baseline (speedup 1.0000x reference)
import numpy as np
import concourse.bass as bass
import concourse.mybir as mybir
from concourse.bass_utils import run_bass_kernel_spmd
from contextlib import ExitStack

F32 = mybir.dt.float32
BF16 = mybir.dt.bfloat16
AF = mybir.ActivationFunctionType
ALU = mybir.AluOpType
AX = mybir.AxisListType

B, S, D = 4, 4096, 2048
HPC = 8
NT = 2048
NG = 8
CSHIFT = 14.0
ISQ = float(1.0 / np.sqrt(128.0))
NCORES = 8

ENGS = ("pe", "act", "dve", "pool", "sp")
SEMS = ("PE", "ACT", "DVE", "GP", "CC")


class Plan:
    def __init__(self, nc, es):
        self.nc = nc
        self.es = es
        self.ops = {e: [] for e in ENGS}
        self.sems = {}
        for name in SEMS:
            self.sems[name] = [es.enter_context(nc.semaphore(name.lower())), 0]

    def qsem(self, name):
        assert name not in self.sems
        self.sems[name] = [self.es.enter_context(self.nc.semaphore(name)), 0]
        return name

    def n(self, s):
        return self.sems[s][1]

    def op(self, eng, fn, waits=(), sig=None, amt=1):
        ws = [(s, v) for (s, v) in waits if v]
        for s, v in ws:
            assert v <= self.sems[s][1], f"wait {s}>={v}, planned {self.sems[s][1]}"
        rec = None
        if sig is not None:
            self.sems[sig][1] += amt
            rec = (sig, amt)
        self.ops[eng].append((ws, fn, rec))
        return self.sems[sig][1] if sig else None

    def play(self, block):
        sems = self.sems

        def body_for(key):
            ops = self.ops[key]

            def body(eng):
                for ws, fn, rec in ops:
                    for s, v in ws:
                        eng.wait_ge(sems[s][0], v)
                    inst = fn(eng)
                    if rec is not None:
                        inst.then_inc(sems[rec[0]][0], rec[1])
            return body
        block.tensor(body_for("pe"))
        block.scalar(body_for("act"))
        block.vector(body_for("dve"))
        block.gpsimd(body_for("pool"))
        block.sync(body_for("sp"))

    def mm(self, out, lhsT, rhs, start, stop, waits=()):
        return self.op("pe", lambda t, o=out, l=lhsT, r=rhs, a=start, b=stop:
                       t.matmul(o, lhsT=l, rhs=r, start=a, stop=b), waits, "PE")

    def transp(self, out, in_, ident, waits=()):
        return self.op("pe", lambda t, o=out, i=in_, d=ident:
                       t.transpose(o, i, d), waits, "PE")

    def dma(self, eng, out, in_, q, waits=(), **kw):
        return self.op(eng, lambda e, o=out, i=in_, k=kw: e.dma_start(out=o, in_=i, **k),
                       waits, q, 16)

    def dmat(self, out, in_, q, waits=()):
        return self.op("act", lambda e, o=out, i=in_: e.dma_start_transpose(o, i),
                       waits, q, 16)

    def act(self, out, in_, func, bias=0.0, scale=1.0, waits=()):
        return self.op("act", lambda e, o=out, i=in_, fn=func, b=bias, sc=scale:
                       e.activation(o, i, fn, bias=b, scale=sc), waits, "ACT")

    def tt(self, out, in0, in1, opr, waits=()):
        return self.op("dve", lambda e, o=out, a=in0, b=in1, p=opr:
                       e.tensor_tensor(o, a, b, op=p), waits, "DVE")

    def ts(self, out, in0, s1, s2, op0, op1, waits=()):
        return self.op("dve", lambda e, o=out, a=in0, x=s1, y=s2, p0=op0, p1=op1:
                       e.tensor_scalar(o, a, x, y, op0=p0, op1=p1), waits, "DVE")

    def stt(self, out, in0, scalar, in1, op0, op1, waits=()):
        return self.op("dve", lambda e, o=out, a=in0, s=scalar, b=in1, p0=op0, p1=op1:
                       e.scalar_tensor_tensor(o, a, s, b, op0=p0, op1=p1), waits, "DVE")


def build_program(taps=(), phases=3):
    nc = bass.Bass("TRN2", target_bir_lowering=False, debug=False)
    es = ExitStack()

    def din(name, shape, dt=F32):
        return nc.dram_tensor(name, shape, dt, kind="ExternalInput").ap()

    x_batch = din("x_batch", [S, D])
    x_local = din("x_local", [NT, D])
    wqkb = din("wqkb", [16, 128, 2048])
    b_qk = din("b_qk", [128, 16])
    wvb = din("wvb", [16, 128, 1024])
    b_v = din("b_v", [1, 1024])
    woutb = din("woutb", [8, 128, 2048])
    b_out = din("b_out", [1, 2048])
    w1tb = din("w1tb", [64, 128, 2240])
    b1h = din("b1h", [128, 64])
    w2tb = din("w2tb", [4, 64, 128, 512])
    b2h = din("b2h", [1, 2048])
    a1tb = din("a1tb", [128, 1024])
    b2rfh = din("b2rfh", [128, 2048])
    wrtb = din("wrtb", [128, 128])
    brh = din("brh", [1, 8])

    out_t = nc.dram_tensor("out", [NT, D], F32, kind="ExternalOutput").ap()

    ln1_tm = nc.dram_tensor("ln1_tm", [S, D], BF16).ap()
    q_buf = nc.dram_tensor("q_buf", [HPC, 128, S], BF16).ap()
    k_buf = nc.dram_tensor("k_buf", [HPC, 128, S], BF16).ap()
    rs_in = nc.dram_tensor("rs_in", [S, D], BF16).ap()
    rs_out = nc.dram_tensor("rs_out", [NT, D], BF16).ap()
    ln2_tm = nc.dram_tensor("ln2_tm", [NT, D], BF16).ap()
    x2_buf = nc.dram_tensor("x2_buf", [NT, D], F32).ap()
    wqkc = nc.dram_tensor("wqkc", [16, 128, 2048], BF16).ap()
    wvc = nc.dram_tensor("wvc", [16, 128, 1024], BF16).ap()
    w1c = nc.dram_tensor("w1c", [64, 128, 2240], BF16).ap()
    w2cs = [nc.dram_tensor(f"w2cs{do}", [32, 128, 1024], BF16).ap() for do in range(4)]
    woutc = nc.dram_tensor("woutc", [8, 128, 2048], BF16).ap()

    tap_src = {"t_ln1": (ln1_tm, [S, D]), "t_q": (q_buf, [HPC, 128, S]),
               "t_k": (k_buf, [HPC, 128, S]), "t_rs": (rs_out, [NT, D]),
               "t_x2": (x2_buf, [NT, D]), "t_ln2": (ln2_tm, [NT, D]),
               "t_rsin": (rs_in, [S, D])}
    tap_aps = {}
    for tname in taps:
        tap_aps[tname] = nc.dram_tensor(tname, tap_src[tname][1], F32,
                                        kind="ExternalOutput").ap()

    P = Plan(nc, es)
    sb = lambda name, shape, dt: es.enter_context(nc.sbuf_tensor(name, shape, dt))

    ident = sb("ident", [128, 128], BF16)
    rep2 = sb("rep2", [8, 64], BF16)
    ones128 = sb("ones128", [128, 1], BF16)
    ones1f = sb("ones1f", [1, 128], F32)
    wrt_sb = sb("wrt_sb", [128, 128], BF16)
    a1t_sb = sb("a1t_sb", [128, 1024], BF16)
    b2rf_sb = sb("b2rf_sb", [128, 2048], BF16)
    w0_st = sb("w0_st", [128, 4], F32)
    w1_st = sb("w1_st", [128, 4], F32)
    r_sc = [sb(f"r_sc{i}", [128, 1], F32) for i in range(3)]
    probs_r = sb("probs_r", [128, 8], F32)
    probs_n = sb("probs_n", [128, 8], F32)
    probs_m = sb("probs_m", [128, 8], F32)
    oh_tm = [sb(f"oh_tm{k}", [128, 8], BF16) for k in range(2)]
    eps_sb = sb("eps_sb", [128, 1], F32)
    csh_sb = sb("csh_sb", [128, 1], F32)
    lnt = [sb(f"lnt{i}", [128, 1], F32) for i in range(2)]
    bn6 = sb("bn6", [128, 24], F32)
    mv = sb("mv", [128, 2], F32)
    rstd = sb("rstd", [128, 1], F32)
    x2_sb = sb("x2_sb", [128, 2048], F32)
    ln2_sb = sb("ln2_sb", [128, 2048], BF16)

    def pl(fn, waits=()):
        return P.op("pool", fn, waits, "GP")

    e0 = pl(lambda g: g.memset(ident[:, :], 1.0))
    e_ident = pl(lambda g: g.affine_select(ident[:, :], ident[:, :], pattern=[[1, 128]],
                                           compare_op=ALU.is_equal, fill=0.0, base=0,
                                           channel_multiplier=-1), [("GP", e0)])
    e0 = pl(lambda g: g.memset(rep2[:, :], 2.0))
    e0 = pl(lambda g: g.affine_select(rep2[:, :], rep2[:, :], pattern=[[1, 64]],
                                      compare_op=ALU.is_ge, fill=0.0, base=0,
                                      channel_multiplier=-8), [("GP", e0)])
    e_rep2 = pl(lambda g: g.affine_select(rep2[:, :], rep2[:, :], pattern=[[-1, 64]],
                                          compare_op=ALU.is_ge, fill=0.0, base=7,
                                          channel_multiplier=8), [("GP", e0)])
    pl(lambda g: g.memset(ones128[:, :], 1.0))
    pl(lambda g: g.memset(ones1f[:, :], 1.0))
    pl(lambda g: g.memset(eps_sb[:, :], 1e-5))
    e_const = pl(lambda g: g.memset(csh_sb[:, :], -CSHIFT))

    SMQ = P.qsem("smq")
    P.dma("pool", wrt_sb[:, :], wrtb, SMQ, waits=[("GP", e_const)])
    P.dma("pool", a1t_sb[:, :], a1tb, SMQ)
    P.dma("pool", b2rf_sb[:, :], b2rfh, SMQ)
    e_smallw = P.n(SMQ)

    s12 = ExitStack()
    vstore = s12.enter_context(nc.sbuf_tensor("vstore", [128, 32 * 1024], BF16))
    s1 = ExitStack()
    sb1 = lambda name, shape, dt: s1.enter_context(nc.sbuf_tensor(name, shape, dt))
    ps1 = [s1.enter_context(nc.psum_tensor(f"ps1_{i}", [128, 512], F32)) for i in range(8)]
    cstage = [sb1(f"cstage{i}", [128, 2240], BF16) for i in range(2)]
    xch = [sb1(f"xch{i}", [128, 2048], F32) for i in range(2)]
    ln1o = [sb1(f"ln1o{i}", [128, 2048], BF16) for i in range(2)]
    xfm = [sb1(f"xfm{i}", [128, 16 * 512], BF16) for i in range(2)]
    wqk_sb = [sb1(f"wqk_sb{i}", [128, 2048], BF16) for i in range(2)]
    wv_sb = sb1("wv_sb", [128, 16 * 512], BF16)
    qkstage = [sb1(f"qkstage{i}", [128, 512], BF16) for i in range(4)]
    b_qk_sb = sb1("b_qk_sb", [128, 16], F32)
    b_v_sb = sb1("b_v_sb", [1, 1024], F32)

    SM2 = P.qsem("sm2")
    P.dma("sp", b_qk_sb[:, :], b_qk, SM2)
    P.dma("sp", b_v_sb[:, :], b_v, SM2)
    e_small = P.n(SM2)

    CG = [P.qsem("cg0"), P.qsem("cg1")]
    CS = [P.qsem("cs0"), P.qsem("cs1")]
    cache_q = []

    def drain_cache(n):
        for _ in range(min(n, len(cache_q))):
            src, dst, w = cache_q.pop(0)
            sl = drain_cache.i % 2
            drain_cache.i += 1
            stg = cstage[sl][0:src.shape[0], 0:w]
            if len(src.shape) == 3:
                stg = stg.rearrange("p (a b) -> p a b", a=src.shape[1])
            P.dma("pool", stg, src, CG[sl], waits=[(CS[sl], P.n(CS[sl]))])
            P.dma("sp", dst, stg, CS[sl], waits=[(CG[sl], P.n(CG[sl]))])
    drain_cache.i = 0

    cache_q += [(wqkb[cc], wqkc[cc], 2048) for cc in range(16)]
    cache_q += [(wvb[dc], wvc[dc], 1024) for dc in range(16)]
    cache_q += [(woutb[hh], woutc[hh], 2048) for hh in range(8)]
    cache_q += [(w1tb[f], w1c[f], 2240) for f in range(64)]
    for do in range(4):
        cache_q += [(w2tb[do, f:f + 2].rearrange("a p b -> p a b"),
                     w2cs[do][f // 2].rearrange("p (a b) -> p a b", a=2), 1024)
                    for f in range(0, 64, 2)]

    XQ = [P.qsem("xq0"), P.qsem("xq1")]
    LW = [P.qsem("lw0"), P.qsem("lw1")]
    e_ln1w = [0] * 32
    xch_free = [0, 0]
    ln1o_free = [0, 0]

    def ln1_chunk(c):
        sl = c % 2
        P.dma("sp", xch[sl][:, :], x_batch[c * 128:(c + 1) * 128, :], XQ[sl],
              waits=[("DVE", xch_free[sl])])
        e_x = P.n(XQ[sl])
        for i in range(4):
            P.op("dve", lambda e, o=bn6[:, i * 6:(i + 1) * 6],
                 s=xch[sl][:, i * 512:(i + 1) * 512]: e.bn_stats(o, s),
                 waits=[(XQ[sl], e_x)] if i == 0 else [], sig="DVE")
        eb = P.n("DVE")
        P.op("dve", lambda e: e.bn_aggr(mv[:, :], bn6[:, :]), waits=[("DVE", eb)],
             sig="DVE")
        ea = P.n("DVE")
        P.tt(lnt[0][:, :], mv[:, 1:2], eps_sb[:, :], ALU.add,
             waits=[("DVE", ea), ("GP", e_const), ("ACT", P.n("ACT"))])
        ev = P.n("DVE")
        el = P.act(lnt[1][:, :], lnt[0][:, :], AF.Ln, waits=[("DVE", ev)])
        er = P.act(rstd[:, :], lnt[1][:, :], AF.Exp, scale=-0.5, waits=[("ACT", el)])
        P.ts(ln1o[sl][:, :], xch[sl][:, :], mv[:, 0:1], rstd[:, :],
             ALU.subtract, ALU.mult, waits=[("ACT", er), (LW[sl], ln1o_free[sl])])
        e_ln = P.n("DVE")
        xch_free[sl] = e_ln
        P.dma("sp", ln1_tm[c * 128:(c + 1) * 128, :], ln1o[sl][:, :], LW[sl],
              waits=[("DVE", e_ln)])
        e_ln1w[c] = (sl, P.n(LW[sl]))
        ln1o_free[sl] = e_ln1w[c][1]

    for c in range(16):
        ln1_chunk(c)
    drain_cache(32)
    e_wqkc = (P.n(CS[0]), P.n(CS[1]))

    TRX = [P.qsem("trx0"), P.qsem("trx1")]
    WQQ = [P.qsem("wqq0"), P.qsem("wqq1")]
    KSQ = [P.qsem(f"ksq{i}") for i in range(4)]
    WVQ = P.qsem("wvq")
    e_kqw = {}
    e_xfm = {}
    bank_free = [(None, 0)] * 8
    stage_free = [0] * 4
    wqk_free = [0, 0]
    xfm_freeable = 0
    e_vmm_prev = 0
    for grp in range(4):
        if grp < 2:
            for c in range(16 + grp * 8, 24 + grp * 8):
                ln1_chunk(c)
        for j in range(2):
            tb = grp * 2 + j
            for dc in range(16):
                w = []
                if dc == 0:
                    w = [(LW[sl_], v_) for (sl_, v_) in
                         (e_ln1w[tb * 4 + i] for i in range(4))]
                    w.append(("PE", xfm_freeable))
                P.dmat(xfm[j][:, dc * 512:(dc + 1) * 512],
                       ln1_tm[tb * 512:(tb + 1) * 512, dc * 128:(dc + 1) * 128],
                       TRX[j], waits=w)
            e_xfm[tb] = P.n(TRX[j])

        for cc in range(16):
            drain_cache(2)
            sl = cc % 2
            P.dma("sp", wqk_sb[sl][:, :], wqkc[cc], WQQ[sl],
                  waits=[(CS[0], e_wqkc[0]), (CS[1], e_wqkc[1]),
                         ("PE", wqk_free[sl])])
            e_w = P.n(WQQ[sl])
            bsel = [(cc % 4) * 2, (cc % 4) * 2 + 1]
            for dc in range(16):
                for j in range(2):
                    tb = grp * 2 + j
                    w = []
                    if dc == 0:
                        w = [(WQQ[sl], e_w), (TRX[j], e_xfm[tb]), bank_free[bsel[j]]]
                    P.mm(ps1[bsel[j]][:, :], wqk_sb[sl][:, dc * 128:(dc + 1) * 128],
                         xfm[j][:, dc * 512:(dc + 1) * 512],
                         start=(dc == 0), stop=(dc == 15), waits=w)
            e_qk = P.n("PE")
            wqk_free[sl] = e_qk
            h = cc // 2
            is_q = (cc % 2 == 0)
            evs = []
            for j in range(2):
                ssl = (cc * 2 + j) % 4
                evs.append(P.act(qkstage[ssl][:, :], ps1[bsel[j]][:, :], AF.Identity,
                                 bias=b_qk_sb[:, cc:cc + 1],
                                 waits=[("PE", e_qk), (SM2, e_small),
                                        (KSQ[ssl], stage_free[ssl])]))
            for j in range(2):
                tb = grp * 2 + j
                ssl = (cc * 2 + j) % 4
                bank_free[bsel[j]] = ("ACT", evs[j])
                dst = (q_buf if is_q else k_buf)[h, :, tb * 512:(tb + 1) * 512]
                P.dma("sp", dst, qkstage[ssl][:, :], KSQ[ssl], waits=[("ACT", evs[j])])
                e_kqw[(h, tb, is_q)] = (ssl, P.n(KSQ[ssl]))
                stage_free[ssl] = P.n(KSQ[ssl])

        for vb in range(2):
            for dc in range(16):
                drain_cache(1)
                P.dma("sp", wv_sb[:, dc * 512:(dc + 1) * 512],
                      wvc[dc][:, vb * 512:(vb + 1) * 512], WVQ,
                      waits=[(CS[0], e_wqkc[0]), (CS[1], e_wqkc[1]),
                             ("PE", e_vmm_prev)] if dc == 0 else [])
            e_wv = P.n(WVQ)
            for tk in range(8):
                j = tk // 4
                tb = grp * 2 + j
                kt_tile = grp * 8 + tk
                bank = 4 + tk % 4
                for dc in range(16):
                    w = []
                    if dc == 0:
                        w = [(WVQ, e_wv), (TRX[j], e_xfm[tb]), bank_free[bank]]
                    P.mm(ps1[bank][:, :],
                         xfm[j][:, dc * 512 + (tk % 4) * 128: dc * 512 + (tk % 4) * 128 + 128],
                         wv_sb[:, dc * 512:(dc + 1) * 512],
                         start=(dc == 0), stop=False, waits=w)
                P.mm(ps1[bank][:, :], ones1f[:, :], b_v_sb[:, vb * 512:(vb + 1) * 512],
                     start=False, stop=True, waits=[(SM2, e_small), ("GP", e_const)])
                e_pv = P.n("PE")
                ev = P.act(vstore[:, kt_tile * 1024 + vb * 512: kt_tile * 1024 + vb * 512 + 512],
                           ps1[bank][:, :], AF.Copy, bias=0.0, waits=[("PE", e_pv)])
                bank_free[bank] = ("ACT", ev)
            e_vmm_prev = P.n("PE")
        xfm_freeable = P.n("PE")

    drain_cache(10 ** 9)
    e_wc = (P.n(CS[0]), P.n(CS[1]))
    e_vdone = P.n("ACT")
    s1.close()

    s2 = ExitStack()
    sb2 = lambda name, shape, dt: s2.enter_context(nc.sbuf_tensor(name, shape, dt))
    psS = [s2.enter_context(nc.psum_tensor(f"psS{i}", [128, 512], F32)) for i in range(2)]
    psC = s2.enter_context(nc.psum_tensor("psC", [128, 512], F32))
    psN = s2.enter_context(nc.psum_tensor("psN", [1, 512], F32))
    psBC = s2.enter_context(nc.psum_tensor("psBC", [128, 512], F32))
    psO = [s2.enter_context(nc.psum_tensor(f"psO{i}", [128, 512], F32)) for i in range(2)]
    diag1 = sb2("diag1", [128, 896], BF16)
    q_sb = [sb2(f"q_sb{i}", [128, 512], BF16) for i in range(2)]
    k_sb = [sb2(f"k_sb{i}", [128, 4096], BF16) for i in range(2)]
    probsE = [sb2(f"probsE{i}", [128, 512], BF16) for i in range(4)]
    probsM = [sb2(f"probsM{i}", [128, 512], BF16) for i in range(2)]
    ctxs = sb2("ctxs", [128, 8 * 512], BF16)
    nbc_sb = sb2("nbc_sb", [128, 512], F32)
    rcp_sb = sb2("rcp_sb", [1, 512], F32)
    wout_sb = sb2("wout_sb", [128, 8 * 2048], BF16)
    ostage = [sb2(f"ostage{i}", [128, 512], BF16) for i in range(2)]
    b_out_sb = sb2("b_out_sb", [1, 2048], F32)
    b_out_h = sb2("b_out_h", [1, 2048], F32)

    BOQ = P.qsem("boq")
    P.dma("sp", b_out_sb[:, :], b_out, BOQ, waits=[("ACT", e_vdone)])
    e_bo = P.n(BOQ)
    e_bouth = P.act(b_out_h[:, :], b_out_sb[:, :], AF.Copy, bias=0.0, scale=0.5,
                    waits=[(BOQ, e_bo)])

    e0 = pl(lambda g: g.memset(diag1[:, :], 1.0),
            [("ACT", e_vdone), (CS[0], e_wc[0]), (CS[1], e_wc[1])])
    e_diag = pl(lambda g: g.affine_select(diag1[:, :], diag1[:, :], pattern=[[1, 896]],
                                          compare_op=ALU.is_ge, fill=0.0, base=-384,
                                          channel_multiplier=-1), [("GP", e0)])

    WOQ = P.qsem("woq")
    for hh in range(HPC):
        P.dma("sp", wout_sb[:, hh * 2048:(hh + 1) * 2048], woutc[hh], WOQ,
              waits=[(CS[0], e_wc[0]), (CS[1], e_wc[1]), ("ACT", e_vdone)]
              if hh == 0 else [])
    e_wout = P.n(WOQ)

    QL = [P.qsem("ql0"), P.qsem("ql1")]
    KL = [P.qsem("kl0"), P.qsem("kl1")]
    OSQ = [P.qsem("osq0"), P.qsem("osq1")]
    PQA = P.qsem("pqa")
    PQB = P.qsem("pqb")

    scoresE = [0, 0]
    probsE_free = [(None, 0)] * 4
    probsM_free = [0] * 2
    q_free = [0, 0]
    k_free = [0, 0]
    ctx_free = 0
    n_free = 0
    nbc_evac = 0
    ctxs_free = 0
    o_free = [0, 0]
    psO_free = [0, 0]
    e_cc = [0] * 8
    e_x2w = [0] * 16
    e_ln2w = [0] * 16
    p3a_done = 0

    def p3a_chunk(c):
        g2 = c // 2
        P.dma("pool", x2_sb[:, :], x_local[c * 128:(c + 1) * 128, :], PQA,
              waits=[("CC", e_cc[g2]), (PQA, e_x2w[c - 1] if c else 0),
                     ("DVE", P.n("DVE"))])
        eg1 = P.n(PQA)
        P.dma("pool", x2_sb[:, :], rs_out[c * 128:(c + 1) * 128, :], PQA,
              waits=[(PQA, eg1)], accum_op=ALU.add)
        e_l = P.n(PQA)
        P.dma("pool", x2_buf[c * 128:(c + 1) * 128, :], x2_sb[:, :], PQA,
              waits=[(PQA, e_l)])
        e_x2w[c] = P.n(PQA)
        for i in range(4):
            P.op("dve", lambda e, o=bn6[:, i * 6:(i + 1) * 6],
                 s=x2_sb[:, i * 512:(i + 1) * 512]: e.bn_stats(o, s),
                 waits=[(PQA, e_l)] if i == 0 else [], sig="DVE")
        eb = P.n("DVE")
        P.op("dve", lambda e: e.bn_aggr(mv[:, :], bn6[:, :]), waits=[("DVE", eb)],
             sig="DVE")
        ea = P.n("DVE")
        P.tt(lnt[0][:, :], mv[:, 1:2], eps_sb[:, :], ALU.add,
             waits=[("DVE", ea), ("ACT", P.n("ACT"))])
        ev = P.n("DVE")
        el = P.act(lnt[1][:, :], lnt[0][:, :], AF.Ln, waits=[("DVE", ev)])
        er = P.act(rstd[:, :], lnt[1][:, :], AF.Exp, scale=-0.5, waits=[("ACT", el)])
        P.ts(ln2_sb[:, :], x2_sb[:, :], mv[:, 0:1], rstd[:, :], ALU.subtract, ALU.mult,
             waits=[("ACT", er), (PQB, e_ln2w[c - 1] if c else 0)])
        e_ts = P.n("DVE")
        P.dma("pool", ln2_tm[c * 128:(c + 1) * 128, :], ln2_sb[:, :], PQB,
              waits=[("DVE", e_ts)])
        e_ln2w[c] = P.n(PQB)

    for g in range(NG):
        ext = 4 * (g + 1)
        for hi in range(HPC):
            qsl = (g * HPC + hi) % 2
            ssl_q, val_q = e_kqw[(hi, g, True)]
            P.dma("sp", q_sb[qsl][:, :], q_buf[hi, :, g * 512:(g + 1) * 512], QL[qsl],
                  waits=[(KSQ[ssl_q], val_q), ("PE", q_free[qsl])])
            e_q = P.n(QL[qsl])
            kw = [(KSQ[i], max((v for (sl_, v) in
                                (e_kqw[(hi, tb, False)] for tb in range(g + 1))
                                if sl_ == i), default=0)) for i in range(4)]
            P.dma("sp", k_sb[qsl][:, 0:ext * 128], k_buf[hi, :, 0:ext * 128], KL[qsl],
                  waits=kw + [("PE", k_free[qsl])])
            e_k = P.n(KL[qsl])

            last_pv = 0
            for i in range(ext):
                bs = i % 2
                w = [(QL[qsl], e_q), (KL[qsl], e_k), ("ACT", scoresE[bs])]
                if g == 0 and hi == 0 and i == 0:
                    w.append(("ACT", e_vdone))
                e_s = P.mm(psS[bs][:, :], k_sb[qsl][:, i * 128:(i + 1) * 128],
                           q_sb[qsl][:, :], start=True, stop=True, waits=w)
                pe_sl = i % 4
                scoresE[bs] = P.act(probsE[pe_sl][:, :], psS[bs][:, :], AF.Exp,
                                    bias=csh_sb[:, 0:1], scale=ISQ,
                                    waits=[("PE", e_s), probsE_free[pe_sl],
                                           ("GP", e_const)])
                e_e = scoresE[bs]
                diag_m = i - 4 * g
                if diag_m >= 0:
                    pm_sl = diag_m % 2
                    off = (3 - diag_m) * 128
                    P.tt(probsM[pm_sl][:, :], probsE[pe_sl][:, :],
                         diag1[:, off:off + 512], ALU.mult,
                         waits=[("ACT", e_e), ("GP", e_diag),
                                ("PE", probsM_free[pm_sl])])
                    pt = probsM[pm_sl]
                    w_pe = [("DVE", P.n("DVE"))]
                else:
                    pt = probsE[pe_sl]
                    w_pe = [("ACT", e_e)]
                w = list(w_pe)
                if i == 0:
                    w += [("DVE", n_free), ("DVE", ctx_free)]
                P.mm(psN[0:1, :], ones128[:, :], pt[:, :], start=(i == 0),
                     stop=(i == ext - 1), waits=w)
                last_pv = P.mm(psC[:, :],
                               vstore[:, i * 1024 + hi * 128: i * 1024 + hi * 128 + 128],
                               pt[:, :], start=(i == 0), stop=(i == ext - 1))
                probsE_free[pe_sl] = (("DVE", P.n("DVE")) if diag_m >= 0
                                      else ("PE", last_pv))
                if diag_m >= 0:
                    probsM_free[pm_sl] = last_pv
            q_free[qsl] = last_pv
            k_free[qsl] = last_pv
            P.op("dve", lambda e: e.reciprocal(rcp_sb[:, :], psN[0:1, :]),
                 waits=[("PE", last_pv), ("DVE", P.n("DVE"))], sig="DVE")
            n_free = P.n("DVE")
            e_bc = P.mm(psBC[:, :], ones1f[:, :], rcp_sb[:, :], start=True, stop=True,
                        waits=[("DVE", n_free), ("DVE", nbc_evac)])
            P.op("dve", lambda e: e.tensor_copy(nbc_sb[:, :], psBC[:, :]),
                 waits=[("PE", e_bc)], sig="DVE")
            nbc_evac = P.n("DVE")
            P.tt(ctxs[:, hi * 512:(hi + 1) * 512], psC[:, :], nbc_sb[:, :], ALU.mult,
                 waits=[("DVE", nbc_evac), ("PE", ctxs_free)])
            ctx_free = P.n("DVE")
        e_ctx_all = P.n("DVE")

        for tc in range(4):
            for do in range(4):
                bank = (tc * 4 + do) % 2
                w = [("DVE", e_ctx_all), (WOQ, e_wout), ("ACT", e_bouth),
                     ("ACT", psO_free[bank])]
                for hh in range(HPC):
                    P.mm(psO[bank][:, :],
                         ctxs[:, hh * 512 + tc * 128: hh * 512 + tc * 128 + 128],
                         wout_sb[:, hh * 2048 + do * 512: hh * 2048 + do * 512 + 512],
                         start=(hh == 0), stop=False, waits=w if hh == 0 else [])
                e_o = P.mm(psO[bank][:, :], ones1f[:, :],
                           b_out_h[0:1, do * 512:(do + 1) * 512], start=False, stop=True)
                osl = (tc * 4 + do) % 2
                psO_free[bank] = P.act(ostage[osl][:, :], psO[bank][:, :], AF.Copy,
                                       bias=0.0,
                                       waits=[("PE", e_o), (OSQ[osl], o_free[osl])])
                P.dma("sp", rs_in[g * 512 + tc * 128: g * 512 + tc * 128 + 128,
                                  do * 512:(do + 1) * 512],
                      ostage[osl][:, :], OSQ[osl], waits=[("ACT", psO_free[bank])])
                o_free[osl] = P.n(OSQ[osl])
        ctxs_free = P.n("PE")
        e_rsw0 = P.n(OSQ[0])
        e_rsw1 = P.n(OSQ[1])

        P.op("pool", lambda gp, a=rs_in[g * 512:(g + 1) * 512, :].opt(),
             b=rs_out[g * 256:(g + 1) * 256, :].opt():
             gp.collective_compute("ReduceScatter", ALU.add,
                                   replica_groups=[[0, 1], [2, 3], [4, 5], [6, 7]],
                                   ins=[a], outs=[b]),
             waits=[(OSQ[0], e_rsw0), (OSQ[1], e_rsw1)], sig="CC")
        e_cc[g] = P.n("CC")
        while p3a_done < 2 * (g - 1):
            p3a_chunk(p3a_done)
            p3a_done += 1
    while p3a_done < 16:
        p3a_chunk(p3a_done)
        p3a_done += 1

    e_p2pe = P.n("PE")
    e_p2act = P.n("ACT")
    e_p2dve = P.n("DVE")
    s2.close()
    s12.close()

    s3 = ExitStack()
    sb3 = lambda name, shape, dt: s3.enter_context(nc.sbuf_tensor(name, shape, dt))
    ps3 = [s3.enter_context(nc.psum_tensor(f"ps3_{i}", [128, 512], F32)) for i in range(6)]
    xfm2 = sb3("xfm2", [128, 16 * 512], BF16)
    w1buf = [sb3(f"w1buf{i}", [128, 2240], BF16) for i in range(2)]
    w2buf = [sb3(f"w2buf{i}", [128, 1024], BF16) for i in range(2)]
    h_sb = [sb3("h0_sb", [128, 64 * 512], BF16), sb3("h1_sb", [128, 64 * 512], BF16)]
    tmpb1 = sb3("tmpb1", [128, 512], F32)
    gin = sb3("gin", [128, 512], F32)
    x2t = sb3("x2t", [128, 512], F32)
    ot = [sb3(f"ot{i}", [128, 512], F32) for i in range(2)]
    ot2 = sb3("ot2", [128, 512], F32)
    b1_sb = sb3("b1_sb", [128, 64], F32)
    b2_sb = sb3("b2_sb", [1, 2048], F32)
    br_sb = sb3("br_sb", [1, 8], F32)
    oh_fm = [sb3(f"oh_fm{k}", [8, 512], BF16) for k in range(2)]
    m1_sb = [sb3(f"m1_sb{k}", [64, 512], BF16) for k in range(2)]
    z1_sb = [sb3(f"z1_sb{k}", [64, 512], BF16) for k in range(2)]
    z2_sb = [sb3(f"z2_sb{k}", [64, 512], BF16) for k in range(2)]

    e_join = [("PE", e_p2pe), ("ACT", e_p2act), ("DVE", e_p2dve)]
    SM3 = P.qsem("sm3")
    P.dma("sp", b1_sb[:, :], b1h, SM3, waits=[("ACT", e_p2act), ("DVE", e_p2dve),
                                              ("PE", e_p2pe)])
    P.dma("sp", b2_sb[:, :], b2h, SM3)
    P.dma("sp", br_sb[:, :], brh, SM3)
    e_small3 = P.n(SM3)

    TR2 = P.qsem("tr2")
    W1S = [P.qsem("w1s0"), P.qsem("w1s1")]
    W2S = [P.qsem("w2s0"), P.qsem("w2s1")]
    X2TQ = P.qsem("x2tq")
    OUTQ = P.qsem("outq")

    xfm2_free = 0
    w1_free = [0, 0]
    w2_free = [0, 0]
    e_tail_dve = 0
    e_x2stt = 0
    ot_free = [0, 0]
    ot2_free = 0

    for t in range(4):
        for dc in range(16):
            w = []
            if dc == 0:
                w = [(PQB, e_ln2w[4 * t + 3]), ("PE", xfm2_free)]
                if t == 0:
                    w += e_join
            P.dmat(xfm2[:, dc * 512:(dc + 1) * 512],
                   ln2_tm[t * 512:(t + 1) * 512, dc * 128:(dc + 1) * 128], TR2,
                   waits=w)
        e_xf = P.n(TR2)

        sps = ExitStack()
        psB16 = sps.enter_context(nc.psum_tensor(f"psB16_{t}", [8, 128], BF16))
        e_m1 = [0, 0]
        e_exp_prev = 0
        for tc in range(4):
            w = [(TR2, e_xf), (SMQ, e_smallw), ("DVE", e_tail_dve),
                 ("ACT", e_exp_prev)]
            if t == 0 and tc == 0:
                w += e_join
            for dc in range(16):
                P.mm(ps3[4][0:128, 0:8],
                     xfm2[:, dc * 512 + tc * 128: dc * 512 + tc * 128 + 128],
                     wrt_sb[:, dc * 8:(dc + 1) * 8], start=(dc == 0), stop=False,
                     waits=w if dc == 0 else [])
            e_r = P.mm(ps3[4][0:128, 0:8], ones1f[:, :], br_sb[:, :], start=False,
                       stop=True, waits=[(SM3, e_small3)])
            P.op("dve", lambda e: e.reduce_max(r_sc[0][:, :], ps3[4][0:128, 0:8],
                                               axis=AX.X, negate=True),
                 waits=[("PE", e_r), ("DVE", P.n("DVE"))], sig="DVE")
            e1 = P.n("DVE")
            e2 = P.act(probs_r[:, :], ps3[4][0:128, 0:8], AF.Exp, bias=r_sc[0][:, :],
                       waits=[("DVE", e1), ("ACT", P.n("ACT"))])
            e_exp_prev = e2
            P.op("dve", lambda e: e.reduce_sum(r_sc[1][:, :], probs_r[:, :], axis=AX.X),
                 waits=[("ACT", e2)], sig="DVE")
            e3 = P.n("DVE")
            P.op("dve", lambda e: e.reciprocal(r_sc[2][:, :], r_sc[1][:, :]),
                 waits=[("DVE", e3)], sig="DVE")
            e4 = P.n("DVE")
            P.ts(probs_n[:, :], probs_r[:, :], r_sc[2][:, :], None, ALU.mult, ALU.bypass,
                 waits=[("DVE", e4)])
            e5 = P.n("DVE")
            P.op("dve", lambda e, o=w0_st[:, tc:tc + 1]:
                 e.reduce_max(o, probs_n[:, :], axis=AX.X), waits=[("DVE", e5)],
                 sig="DVE")
            e6 = P.n("DVE")
            P.ts(oh_tm[0][:, :], probs_n[:, :], w0_st[:, tc:tc + 1], None,
                 ALU.is_equal, ALU.bypass, waits=[("DVE", e6), ("PE", P.n("PE"))])
            e7 = P.n("DVE")
            P.tt(probs_r[:, :], probs_n[:, :], oh_tm[0][:, :], ALU.mult,
                 waits=[("DVE", e7)])
            e7b = P.n("DVE")
            P.tt(probs_m[:, :], probs_n[:, :], probs_r[:, :], ALU.subtract,
                 waits=[("DVE", e7b)])
            e8 = P.n("DVE")
            P.op("dve", lambda e, o=w1_st[:, tc:tc + 1]:
                 e.reduce_max(o, probs_m[:, :], axis=AX.X), waits=[("DVE", e8)],
                 sig="DVE")
            e9 = P.n("DVE")
            P.ts(oh_tm[1][:, :], probs_m[:, :], w1_st[:, tc:tc + 1], None,
                 ALU.is_equal, ALU.bypass, waits=[("DVE", e9), ("PE", P.n("PE"))])
            e10 = P.n("DVE")
            for k in range(2):
                ep = P.transp(psB16[0:8, 0:128], oh_tm[k][:, :], ident[:, :],
                              waits=[("DVE", e10), ("GP", e_ident),
                                     ("ACT", P.n("ACT")), ("DVE", e_tail_dve)])
                P.act(oh_fm[k][:, tc * 128:(tc + 1) * 128], psB16[0:8, 0:128], AF.Copy,
                      bias=0.0, waits=[("PE", ep)])
        e_oh = P.n("ACT")
        for k in range(2):
            em = P.mm(ps3[5][0:64, :], rep2[:, :], oh_fm[k][:, :], start=True, stop=True,
                      waits=[("ACT", e_oh), ("GP", e_rep2), ("DVE", P.n("DVE"))])
            P.op("dve", lambda e, k=k: e.tensor_copy(m1_sb[k][:, :], ps3[5][0:64, :]),
                 waits=[("PE", em)], sig="DVE")
            e_m1[k] = P.n("DVE")
        sps.close()
        psE = ExitStack()
        ps6 = psE.enter_context(nc.psum_tensor(f"ps3_6_{t}", [128, 512], F32))
        ps7 = psE.enter_context(nc.psum_tensor(f"ps3_7_{t}", [128, 512], F32))
        ps8 = [ps3[0], ps3[1], ps3[2], ps3[3], ps3[4], ps3[5], ps6, ps7]

        for dc in range(16):
            P.mm(ps3[5][0:64, :], a1t_sb[:, dc * 64:(dc + 1) * 64],
                 xfm2[:, dc * 512:(dc + 1) * 512], start=(dc == 0), stop=(dc == 15),
                 waits=[(SMQ, e_smallw), ("DVE", e_m1[1])] if dc == 0 else [])
        e_u1 = P.n("PE")
        ez1 = []
        for k in range(2):
            P.tt(z1_sb[k][:, :], ps3[5][0:64, :], m1_sb[k][:, :], ALU.mult,
                 waits=[("PE", e_u1), ("DVE", e_m1[k])])
            ez1.append(P.n("DVE"))

        tmp_free = 0
        gin_free = 0
        e_b1evac = {}
        e_gelu = {}
        e_ttg = {}
        e_u2mm = [0, 0]
        for f in range(64):
            wsl = f % 2
            P.dma("sp", w1buf[wsl][:, :], w1c[f], W1S[wsl],
                  waits=[(CS[0], e_wc[0]), (CS[1], e_wc[1]), ("PE", w1_free[wsl])] +
                        (e_join if (t == 0 and f < 2) else []))
            e_wl = P.n(W1S[wsl])
            bb = f % 2
            for dc in range(16):
                w = []
                if dc == 0:
                    w = [(W1S[wsl], e_wl), (TR2, e_xf), ("ACT", e_b1evac.get(f - 2, 0))]
                    if f == 0:
                        w.append(("DVE", e_m1[1]))
                P.mm(ps3[bb][:, :], w1buf[wsl][:, dc * 128:(dc + 1) * 128],
                     xfm2[:, dc * 512:(dc + 1) * 512], start=(dc == 0), stop=(dc == 15),
                     waits=w)
            e_b1 = P.n("PE")
            e_b1evac[f] = P.act(tmpb1[:, :], ps3[bb][:, :], AF.Identity,
                                bias=b1_sb[:, f:f + 1],
                                waits=[("PE", e_b1), ("DVE", tmp_free),
                                       (SM3, e_small3)])
            for k in range(2):
                bz = 2 + k
                e_zb = P.mm(ps3[bz][:, :], w1buf[wsl][0:64, 2048:2176], z1_sb[k][:, :],
                            start=True, stop=True,
                            waits=[("DVE", ez1[k]), ("DVE", e_ttg.get((f - 1, k), 0))])
                P.tt(gin[:, :], ps3[bz][:, :], tmpb1[:, :], ALU.add,
                     waits=[("PE", e_zb), ("ACT", e_b1evac[f]), ("ACT", gin_free)])
                e_ttg[(f, k)] = P.n("DVE")
                if k == 1:
                    tmp_free = e_ttg[(f, k)]
                e_gelu[(f, k)] = P.act(h_sb[k][:, f * 512:(f + 1) * 512], gin[:, :],
                                       AF.Gelu_apprx_tanh,
                                       waits=[("DVE", e_ttg[(f, k)])])
                gin_free = e_gelu[(f, k)]
                e_u2mm[k] = P.mm(ps3[4 + k][0:64, :], w1buf[wsl][:, 2176:2240],
                                 h_sb[k][:, f * 512:(f + 1) * 512],
                                 start=(f == 0), stop=(f == 63),
                                 waits=[("ACT", e_gelu[(f, k)])])
            w1_free[wsl] = P.n("PE")
        ez2 = []
        for k in range(2):
            P.tt(z2_sb[k][:, :], ps3[4 + k][0:64, :], m1_sb[k][:, :], ALU.mult,
                 waits=[("PE", e_u2mm[k]), ("DVE", e_m1[k])])
            ez2.append(P.n("DVE"))
        e_b1last = e_b1evac[63]
        e_ttglast = e_ttg[(63, 1)]

        e_stt2 = {}
        for do in range(4):
            for tc in range(4):
                for k in range(2):
                    bank = ps8[k * 4 + tc]
                    w = [(SM3, e_small3), ("DVE", ez2[k])]
                    if do == 0:
                        w += [("ACT", e_b1last), ("DVE", e_ttglast), ("DVE", ez2[1]),
                              ("DVE", e_tail_dve)]
                    else:
                        w.append(("DVE", e_stt2[(do - 1, tc)]))
                    P.mm(bank[:, :], ones1f[:, :], b2_sb[0:1, do * 512:(do + 1) * 512],
                         start=True, stop=False, waits=w)
                    P.mm(bank[:, :], z2_sb[k][:, tc * 128:(tc + 1) * 128],
                         b2rf_sb[0:64, do * 512:(do + 1) * 512], start=False, stop=False)
            for fq in range(32):
                wsl = fq % 2
                P.dma("sp", w2buf[wsl][:, :], w2cs[do][fq], W2S[wsl],
                      waits=[(CS[0], e_wc[0]), (CS[1], e_wc[1]),
                             ("PE", w2_free[wsl])] +
                            (e_join if (t == 0 and do == 0 and fq < 2) else []))
                e_w2 = P.n(W2S[wsl])
                for fi in range(2):
                    f = 2 * fq + fi
                    for k in range(2):
                        for tc in range(4):
                            w = []
                            if fi == 0 and k == 0 and tc == 0:
                                w = [(W2S[wsl], e_w2), ("ACT", e_gelu[(63, 1)])]
                            P.mm(ps8[k * 4 + tc][:, :],
                                 h_sb[k][:, f * 512 + tc * 128: f * 512 + tc * 128 + 128],
                                 w2buf[wsl][:, fi * 512:(fi + 1) * 512],
                                 start=False, stop=(f == 63), waits=w)
                w2_free[wsl] = P.n("PE")
            e_stop = P.n("PE")

            for tc in range(4):
                P.dma("sp", x2t[:, :],
                      x2_buf[(t * 4 + tc) * 128:(t * 4 + tc) * 128 + 128,
                             do * 512:(do + 1) * 512], X2TQ,
                      waits=[(PQA, e_x2w[t * 4 + tc]), ("DVE", e_x2stt)])
                e_xl = P.n(X2TQ)
                osl = (do * 4 + tc) % 2
                P.stt(ot[osl][:, :], ps8[tc][:, :], w0_st[:, tc:tc + 1],
                      x2t[:, :], ALU.mult, ALU.add,
                      waits=[("PE", e_stop), (X2TQ, e_xl), ("DVE", ot_free[osl])])
                e_x2stt = P.n("DVE")
                P.stt(ot2[:, :], ps8[4 + tc][:, :], w1_st[:, tc:tc + 1],
                      ot[osl][:, :], ALU.mult, ALU.add,
                      waits=[("DVE", e_x2stt), (OUTQ, ot2_free)])
                e_stt2[(do, tc)] = P.n("DVE")
                ot_free[osl] = e_stt2[(do, tc)]
                P.dma("sp", out_t[(t * 4 + tc) * 128:(t * 4 + tc) * 128 + 128,
                                  do * 512:(do + 1) * 512],
                      ot2[:, :], OUTQ, waits=[("DVE", e_stt2[(do, tc)])])
                ot2_free = P.n(OUTQ)
        e_tail_dve = P.n("DVE")
        xfm2_free = P.n("PE")
        psE.close()

    if taps:
        TAPQ = P.qsem("tapq")
        allq = [(name, P.n(name)) for name in P.sems if name not in ("CC",)]
        for tname in taps:
            src = tap_src[tname][0]
            P.dma("pool", tap_aps[tname], src[tuple(slice(None) for _ in src.shape)],
                  TAPQ, waits=allq + [("CC", P.n("CC"))])
            allq = []
        P.op("sp", lambda e: e.nop(),
             waits=[(TAPQ, P.n(TAPQ)), (OUTQ, P.n(OUTQ))], sig=None)
    else:
        P.op("sp", lambda e: e.nop(), waits=[(OUTQ, P.n(OUTQ))], sig=None)

    s3.close()
    with nc.Block() as block:
        P.play(block)
    es.close()
    return nc


def _local_rows(r):
    return np.concatenate([np.arange(g * 512 + r * 256, g * 512 + r * 256 + 256)
                           for g in range(8)])


def _prep_core_inputs(c, I):
    b, r = c // 2, c % 2
    A = lambda x: np.ascontiguousarray(x, dtype=np.float32)
    X = np.asarray(I["L0_attn_x"], dtype=np.float32)
    w_qkv = np.asarray(I["L0_attn_w_qkv"], dtype=np.float32)
    heads = range(r * 8, r * 8 + 8)
    qk_cols = np.concatenate([np.r_[h * 384:h * 384 + 128, h * 384 + 128:h * 384 + 256]
                              for h in heads])
    v_cols = np.concatenate([np.r_[h * 384 + 256:h * 384 + 384] for h in heads])
    w_qk = w_qkv[:, qk_cols]
    d = {}
    d["x_batch"] = A(X[b])
    d["x_local"] = A(X[b][_local_rows(r)])
    d["wqkb"] = A(w_qk.reshape(16, 128, 16, 128).transpose(2, 1, 0, 3).reshape(16, 128, 2048))
    b_qkv = np.asarray(I["L0_attn_b_qkv"], dtype=np.float32)
    d["b_qk"] = A(b_qkv[qk_cols].reshape(16, 128).T)
    d["wvb"] = A(w_qkv[:, v_cols].reshape(16, 128, 1024))
    d["b_v"] = A(b_qkv[v_cols].reshape(1, 1024))
    w_out = np.asarray(I["L0_attn_w_out"], dtype=np.float32)
    d["woutb"] = A(w_out[r * 1024:(r + 1) * 1024].reshape(8, 128, 2048))
    d["b_out"] = A(np.asarray(I["L0_attn_b_out"], np.float32).reshape(1, 2048))
    W1 = np.asarray(I["L0_ffn_W1"], dtype=np.float32)
    w1tb = np.zeros((64, 128, 2240), np.float32)
    w1tb[:, :, 0:2048] = W1.reshape(64, 128, 16, 128).transpose(0, 3, 2, 1).reshape(64, 128, 2048)
    B1rf = np.asarray(I["L0_ffn_B1"], np.float32).transpose(0, 2, 1).reshape(64, 8192)
    w1tb[:, 0:64, 2048:2176] = B1rf.reshape(64, 64, 128).transpose(1, 0, 2)
    A2rf = np.asarray(I["L0_ffn_A2"], np.float32).reshape(64, 8192)
    w1tb[:, :, 2176:2240] = A2rf.T.reshape(64, 128, 64)
    d["w1tb"] = A(w1tb)
    d["b1h"] = A(np.asarray(I["L0_ffn_b1"], np.float32).reshape(64, 128).T)
    W2T = np.asarray(I["L0_ffn_W2"], np.float32).T
    d["w2tb"] = A(W2T.reshape(64, 128, 4, 512).transpose(2, 0, 1, 3))
    d["b2h"] = A(np.asarray(I["L0_ffn_b2"], np.float32).reshape(1, 2048))
    A1rf = np.asarray(I["L0_ffn_A1"], np.float32).reshape(64, 2048)
    d["a1tb"] = A(A1rf.T.reshape(16, 128, 64).transpose(1, 0, 2).reshape(128, 1024))
    b2rf = np.zeros((128, 2048), np.float32)
    b2rf[0:64] = np.asarray(I["L0_ffn_B2"], np.float32).transpose(0, 2, 1).reshape(64, 2048)
    d["b2rfh"] = A(b2rf)
    d["wrtb"] = A(np.asarray(I["L0_ffn_W_r"], np.float32).T.reshape(16, 128, 8)
                  .transpose(1, 0, 2).reshape(128, 128))
    d["brh"] = A(np.asarray(I["L0_ffn_b_r"], np.float32).reshape(1, 8))
    return d


def kernel(**inputs):
    nc = build_program()
    in_maps = [_prep_core_inputs(c, inputs) for c in range(NCORES)]
    res = run_bass_kernel_spmd(nc, in_maps, core_ids=list(range(NCORES)))
    out = np.empty((B, S, D), np.float32)
    for c in range(NCORES):
        b, r = c // 2, c % 2
        out[b][_local_rows(r)] = res.results[c]["out"]
    return out


# version 02
# speedup vs baseline: 1.0317x; 1.0317x over previous
import numpy as np
import concourse.bass as bass
import concourse.mybir as mybir
from concourse.bass_utils import run_bass_kernel_spmd
from contextlib import ExitStack

F32 = mybir.dt.float32
BF16 = mybir.dt.bfloat16
AF = mybir.ActivationFunctionType
ALU = mybir.AluOpType
AX = mybir.AxisListType

B, S, D = 4, 4096, 2048
HPC = 8
NT = 2048
NG = 8
CSHIFT = 14.0
ISQ = float(1.0 / np.sqrt(128.0))
NCORES = 8

ENGS = ("pe", "act", "dve", "pool", "sp")
SEMS = ("PE", "ACT", "DVE", "GP", "CC")


class Plan:
    def __init__(self, nc, es):
        self.nc = nc
        self.es = es
        self.ops = {e: [] for e in ENGS}
        self.sems = {}
        for name in SEMS:
            self.sems[name] = [es.enter_context(nc.semaphore(name.lower())), 0]

    def qsem(self, name):
        assert name not in self.sems
        self.sems[name] = [self.es.enter_context(self.nc.semaphore(name)), 0]
        return name

    def n(self, s):
        return self.sems[s][1]

    def op(self, eng, fn, waits=(), sig=None, amt=1):
        ws = [(s, v) for (s, v) in waits if v]
        for s, v in ws:
            assert v <= self.sems[s][1], f"wait {s}>={v}, planned {self.sems[s][1]}"
        rec = None
        if sig is not None:
            self.sems[sig][1] += amt
            rec = (sig, amt)
        self.ops[eng].append((ws, fn, rec))
        return self.sems[sig][1] if sig else None

    def play(self, block):
        sems = self.sems

        def body_for(key):
            ops = self.ops[key]

            def body(eng):
                for ws, fn, rec in ops:
                    for s, v in ws:
                        eng.wait_ge(sems[s][0], v)
                    inst = fn(eng)
                    if rec is not None:
                        inst.then_inc(sems[rec[0]][0], rec[1])
            return body
        block.tensor(body_for("pe"))
        block.scalar(body_for("act"))
        block.vector(body_for("dve"))
        block.gpsimd(body_for("pool"))
        block.sync(body_for("sp"))

    def mm(self, out, lhsT, rhs, start, stop, waits=()):
        return self.op("pe", lambda t, o=out, l=lhsT, r=rhs, a=start, b=stop:
                       t.matmul(o, lhsT=l, rhs=r, start=a, stop=b), waits, "PE")

    def transp(self, out, in_, ident, waits=()):
        return self.op("pe", lambda t, o=out, i=in_, d=ident:
                       t.transpose(o, i, d), waits, "PE")

    def dma(self, eng, out, in_, q, waits=(), **kw):
        return self.op(eng, lambda e, o=out, i=in_, k=kw: e.dma_start(out=o, in_=i, **k),
                       waits, q, 16)

    def dmat(self, out, in_, q, waits=()):
        return self.op("act", lambda e, o=out, i=in_: e.dma_start_transpose(o, i),
                       waits, q, 16)

    def act(self, out, in_, func, bias=0.0, scale=1.0, waits=()):
        return self.op("act", lambda e, o=out, i=in_, fn=func, b=bias, sc=scale:
                       e.activation(o, i, fn, bias=b, scale=sc), waits, "ACT")

    def tt(self, out, in0, in1, opr, waits=()):
        return self.op("dve", lambda e, o=out, a=in0, b=in1, p=opr:
                       e.tensor_tensor(o, a, b, op=p), waits, "DVE")

    def ts(self, out, in0, s1, s2, op0, op1, waits=()):
        return self.op("dve", lambda e, o=out, a=in0, x=s1, y=s2, p0=op0, p1=op1:
                       e.tensor_scalar(o, a, x, y, op0=p0, op1=p1), waits, "DVE")

    def stt(self, out, in0, scalar, in1, op0, op1, waits=()):
        return self.op("dve", lambda e, o=out, a=in0, s=scalar, b=in1, p0=op0, p1=op1:
                       e.scalar_tensor_tensor(o, a, s, b, op0=p0, op1=p1), waits, "DVE")


def build_program(taps=(), phases=3):
    nc = bass.Bass("TRN2", target_bir_lowering=False, debug=False)
    es = ExitStack()

    def din(name, shape, dt=F32):
        return nc.dram_tensor(name, shape, dt, kind="ExternalInput").ap()

    x_batch = din("x_batch", [S, D])
    x_local = din("x_local", [NT, D])
    wqkb = din("wqkb", [16, 128, 2048])
    b_qk = din("b_qk", [128, 16])
    wvb = din("wvb", [16, 128, 1024])
    b_v = din("b_v", [1, 1024])
    woutb = din("woutb", [8, 128, 2048])
    b_out = din("b_out", [1, 2048])
    w1tb = din("w1tb", [64, 128, 2240])
    b1h = din("b1h", [128, 64])
    w2tb = din("w2tb", [4, 64, 128, 512])
    b2h = din("b2h", [1, 2048])
    a1tb = din("a1tb", [128, 1024])
    b2rfh = din("b2rfh", [128, 2048])
    wrtb = din("wrtb", [128, 128])
    brh = din("brh", [1, 8])

    out_t = nc.dram_tensor("out", [NT, D], F32, kind="ExternalOutput").ap()

    ln1_tm = nc.dram_tensor("ln1_tm", [S, D], BF16).ap()
    q_buf = nc.dram_tensor("q_buf", [HPC, 128, S], BF16).ap()
    k_buf = nc.dram_tensor("k_buf", [HPC, 128, S], BF16).ap()
    rs_in = nc.dram_tensor("rs_in", [S, D], BF16).ap()
    rs_out = nc.dram_tensor("rs_out", [NT, D], BF16).ap()
    ln2_tm = nc.dram_tensor("ln2_tm", [NT, D], BF16).ap()
    x2_buf = nc.dram_tensor("x2_buf", [NT, D], F32).ap()
    wqkc = nc.dram_tensor("wqkc", [16, 128, 2048], BF16).ap()
    wvc = nc.dram_tensor("wvc", [16, 128, 1024], BF16).ap()
    w1c = nc.dram_tensor("w1c", [64, 128, 2240], BF16).ap()
    w2cs = [nc.dram_tensor(f"w2cs{do}", [32, 128, 1024], BF16).ap() for do in range(4)]
    woutc = nc.dram_tensor("woutc", [8, 128, 2048], BF16).ap()

    tap_src = {"t_ln1": (ln1_tm, [S, D]), "t_q": (q_buf, [HPC, 128, S]),
               "t_k": (k_buf, [HPC, 128, S]), "t_rs": (rs_out, [NT, D]),
               "t_x2": (x2_buf, [NT, D]), "t_ln2": (ln2_tm, [NT, D]),
               "t_rsin": (rs_in, [S, D])}
    tap_aps = {}
    for tname in taps:
        tap_aps[tname] = nc.dram_tensor(tname, tap_src[tname][1], F32,
                                        kind="ExternalOutput").ap()

    P = Plan(nc, es)
    sb = lambda name, shape, dt: es.enter_context(nc.sbuf_tensor(name, shape, dt))

    ident = sb("ident", [128, 128], BF16)
    rep2 = sb("rep2", [8, 64], BF16)
    ones128 = sb("ones128", [128, 1], BF16)
    ones1f = sb("ones1f", [1, 128], F32)
    wrt_sb = sb("wrt_sb", [128, 128], BF16)
    a1t_sb = sb("a1t_sb", [128, 1024], BF16)
    b2rf_sb = sb("b2rf_sb", [128, 2048], BF16)
    w0_st = sb("w0_st", [128, 4], F32)
    w1_st = sb("w1_st", [128, 4], F32)
    r_sc = [sb(f"r_sc{i}", [128, 1], F32) for i in range(3)]
    probs_r = sb("probs_r", [128, 8], F32)
    probs_n = sb("probs_n", [128, 8], F32)
    probs_m = sb("probs_m", [128, 8], F32)
    oh_tm = [sb(f"oh_tm{k}", [128, 8], BF16) for k in range(2)]
    eps_sb = sb("eps_sb", [128, 1], F32)
    csh_sb = sb("csh_sb", [128, 1], F32)
    lnt = [sb(f"lnt{i}", [128, 1], F32) for i in range(2)]
    bn6 = sb("bn6", [128, 24], F32)
    mv = sb("mv", [128, 2], F32)
    rstd = sb("rstd", [128, 1], F32)

    def pl(fn, waits=()):
        return P.op("pool", fn, waits, "GP")

    e0 = pl(lambda g: g.memset(ident[:, :], 1.0))
    e_ident = pl(lambda g: g.affine_select(ident[:, :], ident[:, :], pattern=[[1, 128]],
                                           compare_op=ALU.is_equal, fill=0.0, base=0,
                                           channel_multiplier=-1), [("GP", e0)])
    e0 = pl(lambda g: g.memset(rep2[:, :], 2.0))
    e0 = pl(lambda g: g.affine_select(rep2[:, :], rep2[:, :], pattern=[[1, 64]],
                                      compare_op=ALU.is_ge, fill=0.0, base=0,
                                      channel_multiplier=-8), [("GP", e0)])
    e_rep2 = pl(lambda g: g.affine_select(rep2[:, :], rep2[:, :], pattern=[[-1, 64]],
                                          compare_op=ALU.is_ge, fill=0.0, base=7,
                                          channel_multiplier=8), [("GP", e0)])
    pl(lambda g: g.memset(ones128[:, :], 1.0))
    pl(lambda g: g.memset(ones1f[:, :], 1.0))
    pl(lambda g: g.memset(eps_sb[:, :], 1e-5))
    e_const = pl(lambda g: g.memset(csh_sb[:, :], -CSHIFT))

    SMQ = P.qsem("smq")
    P.dma("pool", wrt_sb[:, :], wrtb, SMQ, waits=[("GP", e_const)])
    P.dma("pool", a1t_sb[:, :], a1tb, SMQ)
    P.dma("pool", b2rf_sb[:, :], b2rfh, SMQ)
    e_smallw = P.n(SMQ)

    s12 = ExitStack()
    vstore = s12.enter_context(nc.sbuf_tensor("vstore", [128, 32 * 1024], BF16))
    s1 = ExitStack()
    sb1 = lambda name, shape, dt: s1.enter_context(nc.sbuf_tensor(name, shape, dt))
    ps1 = [s1.enter_context(nc.psum_tensor(f"ps1_{i}", [128, 512], F32)) for i in range(8)]
    cstage = [sb1(f"cstage{i}", [128, 2240], BF16) for i in range(2)]
    xch = [sb1(f"xch{i}", [128, 2048], F32) for i in range(2)]
    ln1o = [sb1(f"ln1o{i}", [128, 2048], BF16) for i in range(2)]
    xfm = [sb1(f"xfm{i}", [128, 16 * 512], BF16) for i in range(2)]
    wqk_sb = [sb1(f"wqk_sb{i}", [128, 2048], BF16) for i in range(2)]
    wv_sb = sb1("wv_sb", [128, 16 * 512], BF16)
    qkstage = [sb1(f"qkstage{i}", [128, 512], BF16) for i in range(4)]
    b_qk_sb = sb1("b_qk_sb", [128, 16], F32)
    b_v_sb = sb1("b_v_sb", [1, 1024], F32)

    SM2 = P.qsem("sm2")
    P.dma("sp", b_qk_sb[:, :], b_qk, SM2)
    P.dma("sp", b_v_sb[:, :], b_v, SM2)
    e_small = P.n(SM2)

    CG = [P.qsem("cg0"), P.qsem("cg1")]
    CS = [P.qsem("cs0"), P.qsem("cs1")]
    cache_q = []

    def drain_cache(n):
        for _ in range(min(n, len(cache_q))):
            src, dst, w = cache_q.pop(0)
            sl = drain_cache.i % 2
            drain_cache.i += 1
            stg = cstage[sl][0:src.shape[0], 0:w]
            if len(src.shape) == 3:
                stg = stg.rearrange("p (a b) -> p a b", a=src.shape[1])
            P.dma("pool", stg, src, CG[sl], waits=[(CS[sl], P.n(CS[sl]))])
            P.dma(["sp", "act"][sl], dst, stg, CS[sl], waits=[(CG[sl], P.n(CG[sl]))])
    drain_cache.i = 0

    cache_q += [(wqkb[cc], wqkc[cc], 2048) for cc in range(16)]
    cache_q += [(wvb[dc], wvc[dc], 1024) for dc in range(16)]
    cache_q += [(woutb[hh], woutc[hh], 2048) for hh in range(8)]
    cache_q += [(w1tb[f], w1c[f], 2240) for f in range(64)]
    for do in range(4):
        cache_q += [(w2tb[do, f:f + 2].rearrange("a p b -> p a b"),
                     w2cs[do][f // 2].rearrange("p (a b) -> p a b", a=2), 1024)
                    for f in range(0, 64, 2)]

    XQ = [P.qsem("xq0"), P.qsem("xq1")]
    LW = [P.qsem("lw0"), P.qsem("lw1")]
    e_ln1w = [0] * 32
    xch_free = [0, 0]
    ln1o_free = [0, 0]

    def ln1_chunk(c):
        sl = c % 2
        P.dma(["sp", "act"][sl], xch[sl][:, :], x_batch[c * 128:(c + 1) * 128, :],
              XQ[sl], waits=[("DVE", xch_free[sl])])
        e_x = P.n(XQ[sl])
        for i in range(4):
            P.op("dve", lambda e, o=bn6[:, i * 6:(i + 1) * 6],
                 s=xch[sl][:, i * 512:(i + 1) * 512]: e.bn_stats(o, s),
                 waits=[(XQ[sl], e_x)] if i == 0 else [], sig="DVE")
        eb = P.n("DVE")
        P.op("dve", lambda e: e.bn_aggr(mv[:, :], bn6[:, :]), waits=[("DVE", eb)],
             sig="DVE")
        ea = P.n("DVE")
        P.tt(lnt[0][:, :], mv[:, 1:2], eps_sb[:, :], ALU.add,
             waits=[("DVE", ea), ("GP", e_const), ("ACT", P.n("ACT"))])
        ev = P.n("DVE")
        el = P.act(lnt[1][:, :], lnt[0][:, :], AF.Ln, waits=[("DVE", ev)])
        er = P.act(rstd[:, :], lnt[1][:, :], AF.Exp, scale=-0.5, waits=[("ACT", el)])
        P.ts(ln1o[sl][:, :], xch[sl][:, :], mv[:, 0:1], rstd[:, :],
             ALU.subtract, ALU.mult, waits=[("ACT", er), (LW[sl], ln1o_free[sl])])
        e_ln = P.n("DVE")
        xch_free[sl] = e_ln
        P.dma("sp", ln1_tm[c * 128:(c + 1) * 128, :], ln1o[sl][:, :], LW[sl],
              waits=[("DVE", e_ln)])
        e_ln1w[c] = (sl, P.n(LW[sl]))
        ln1o_free[sl] = e_ln1w[c][1]

    for c in range(16):
        ln1_chunk(c)
    drain_cache(32)
    e_wqkc = (P.n(CS[0]), P.n(CS[1]))

    TRX = [P.qsem("trx0"), P.qsem("trx1")]
    WQQ = [P.qsem("wqq0"), P.qsem("wqq1")]
    KSQ = [P.qsem(f"ksq{i}") for i in range(4)]
    WVQ = P.qsem("wvq")
    e_kqw = {}
    e_xfm = {}
    bank_free = [(None, 0)] * 8
    stage_free = [0] * 4
    wqk_free = [0, 0]
    xfm_freeable = 0
    e_vmm_prev = 0
    for grp in range(4):
        if grp < 2:
            for c in range(16 + grp * 8, 24 + grp * 8):
                ln1_chunk(c)
        for j in range(2):
            tb = grp * 2 + j
            for dc in range(16):
                w = []
                if dc == 0:
                    w = [(LW[sl_], v_) for (sl_, v_) in
                         (e_ln1w[tb * 4 + i] for i in range(4))]
                    w.append(("PE", xfm_freeable))
                P.dmat(xfm[j][:, dc * 512:(dc + 1) * 512],
                       ln1_tm[tb * 512:(tb + 1) * 512, dc * 128:(dc + 1) * 128],
                       TRX[j], waits=w)
            e_xfm[tb] = P.n(TRX[j])

        for cc in range(16):
            drain_cache(2)
            sl = cc % 2
            P.dma(["sp", "act"][sl], wqk_sb[sl][:, :], wqkc[cc], WQQ[sl],
                  waits=[(CS[0], e_wqkc[0]), (CS[1], e_wqkc[1]),
                         ("PE", wqk_free[sl])])
            e_w = P.n(WQQ[sl])
            bsel = [(cc % 4) * 2, (cc % 4) * 2 + 1]
            for dc in range(16):
                for j in range(2):
                    tb = grp * 2 + j
                    w = []
                    if dc == 0:
                        w = [(WQQ[sl], e_w), (TRX[j], e_xfm[tb]), bank_free[bsel[j]]]
                    P.mm(ps1[bsel[j]][:, :], wqk_sb[sl][:, dc * 128:(dc + 1) * 128],
                         xfm[j][:, dc * 512:(dc + 1) * 512],
                         start=(dc == 0), stop=(dc == 15), waits=w)
            e_qk = P.n("PE")
            wqk_free[sl] = e_qk
            h = cc // 2
            is_q = (cc % 2 == 0)
            evs = []
            for j in range(2):
                ssl = (cc * 2 + j) % 4
                evs.append(P.act(qkstage[ssl][:, :], ps1[bsel[j]][:, :], AF.Identity,
                                 bias=b_qk_sb[:, cc:cc + 1],
                                 waits=[("PE", e_qk), (SM2, e_small),
                                        (KSQ[ssl], stage_free[ssl])]))
            for j in range(2):
                tb = grp * 2 + j
                ssl = (cc * 2 + j) % 4
                bank_free[bsel[j]] = ("ACT", evs[j])
                dst = (q_buf if is_q else k_buf)[h, :, tb * 512:(tb + 1) * 512]
                P.dma(["sp", "sp", "act", "act"][ssl], dst, qkstage[ssl][:, :],
                      KSQ[ssl], waits=[("ACT", evs[j])])
                e_kqw[(h, tb, is_q)] = (ssl, P.n(KSQ[ssl]))
                stage_free[ssl] = P.n(KSQ[ssl])

        for vb in range(2):
            for dc in range(16):
                drain_cache(1)
                P.dma(["sp", "act"][dc % 2], wv_sb[:, dc * 512:(dc + 1) * 512],
                      wvc[dc][:, vb * 512:(vb + 1) * 512], WVQ,
                      waits=[(CS[0], e_wqkc[0]), (CS[1], e_wqkc[1]),
                             ("PE", e_vmm_prev)] if dc < 2 else [])
            e_wv = P.n(WVQ)
            for tk in range(8):
                j = tk // 4
                tb = grp * 2 + j
                kt_tile = grp * 8 + tk
                bank = 4 + tk % 4
                for dc in range(16):
                    w = []
                    if dc == 0:
                        w = [(WVQ, e_wv), (TRX[j], e_xfm[tb]), bank_free[bank]]
                    P.mm(ps1[bank][:, :],
                         xfm[j][:, dc * 512 + (tk % 4) * 128: dc * 512 + (tk % 4) * 128 + 128],
                         wv_sb[:, dc * 512:(dc + 1) * 512],
                         start=(dc == 0), stop=False, waits=w)
                P.mm(ps1[bank][:, :], ones1f[:, :], b_v_sb[:, vb * 512:(vb + 1) * 512],
                     start=False, stop=True, waits=[(SM2, e_small), ("GP", e_const)])
                e_pv = P.n("PE")
                ev = P.act(vstore[:, kt_tile * 1024 + vb * 512: kt_tile * 1024 + vb * 512 + 512],
                           ps1[bank][:, :], AF.Copy, bias=0.0, waits=[("PE", e_pv)])
                bank_free[bank] = ("ACT", ev)
            e_vmm_prev = P.n("PE")
        xfm_freeable = P.n("PE")

    drain_cache(10 ** 9)
    e_wc = (P.n(CS[0]), P.n(CS[1]))
    e_vdone = P.n("ACT")
    s1.close()

    s2 = ExitStack()
    sb2 = lambda name, shape, dt: s2.enter_context(nc.sbuf_tensor(name, shape, dt))
    psS = [s2.enter_context(nc.psum_tensor(f"psS{i}", [128, 512], F32)) for i in range(2)]
    psCs = [s2.enter_context(nc.psum_tensor(f"psC{i}", [128, 512], F32)) for i in range(2)]
    psN = s2.enter_context(nc.psum_tensor("psN", [1, 512], F32))
    psBC = s2.enter_context(nc.psum_tensor("psBC", [128, 512], F32))
    psO = [s2.enter_context(nc.psum_tensor(f"psO{i}", [128, 512], F32)) for i in range(2)]
    psNs = [psN[0:1, :], psO[0][0:1, :]]
    diag1 = sb2("diag1", [128, 896], BF16)
    q_sb = [sb2(f"q_sb{i}", [128, 512], BF16) for i in range(2)]
    k_sb = [sb2(f"k_sb{i}", [128, 4096], BF16) for i in range(2)]
    probsE = [sb2(f"probsE{i}", [128, 512], BF16) for i in range(4)]
    probsM = [sb2(f"probsM{i}", [128, 512], BF16) for i in range(2)]
    ctxs = sb2("ctxs", [128, 8 * 512], BF16)
    nbc_sb = sb2("nbc_sb", [128, 512], F32)
    rcp_sb = sb2("rcp_sb", [1, 512], F32)
    wout_sb = sb2("wout_sb", [128, 8 * 2048], BF16)
    ostage = [sb2(f"ostage{i}", [128, 512], BF16) for i in range(2)]
    b_out_sb = sb2("b_out_sb", [1, 2048], F32)
    b_out_h = sb2("b_out_h", [1, 2048], F32)
    x2_sb = sb2("x2_sb", [128, 2048], F32)
    ln2_sb = sb2("ln2_sb", [128, 2048], BF16)

    BOQ = P.qsem("boq")
    P.dma("sp", b_out_sb[:, :], b_out, BOQ, waits=[("ACT", e_vdone)])
    e_bo = P.n(BOQ)
    e_bouth = P.act(b_out_h[:, :], b_out_sb[:, :], AF.Copy, bias=0.0, scale=0.5,
                    waits=[(BOQ, e_bo)])

    e0 = pl(lambda g: g.memset(diag1[:, :], 1.0),
            [("ACT", e_vdone), (CS[0], e_wc[0]), (CS[1], e_wc[1])])
    e_diag = pl(lambda g: g.affine_select(diag1[:, :], diag1[:, :], pattern=[[1, 896]],
                                          compare_op=ALU.is_ge, fill=0.0, base=-384,
                                          channel_multiplier=-1), [("GP", e0)])

    WOQ = P.qsem("woq")
    for hh in range(HPC):
        P.dma("sp", wout_sb[:, hh * 2048:(hh + 1) * 2048], woutc[hh], WOQ,
              waits=[(CS[0], e_wc[0]), (CS[1], e_wc[1]), ("ACT", e_vdone)]
              if hh == 0 else [])
    e_wout = P.n(WOQ)

    QL = [P.qsem("ql0"), P.qsem("ql1")]
    KL = [P.qsem("kl0"), P.qsem("kl1")]
    OSQ = [P.qsem("osq0"), P.qsem("osq1")]
    PQA = P.qsem("pqa")
    PQB = P.qsem("pqb")

    scoresE = [0, 0]
    probsE_free = [(None, 0)] * 4
    probsM_free = [0] * 2
    q_free = [0, 0]
    k_free = [0, 0]
    ctx_free2 = [0, 0]
    n_free2 = [0, 0]
    nbc_evac = 0
    ctxs_free = 0
    o_free = [0, 0]
    psO_free = [0, 0]
    e_cc = [0] * 8
    e_x2w = [0] * 16
    e_ln2w = [0] * 16
    p3a_done = 0
    pending_tail = [None]

    def p3a_chunk(c):
        g2 = c // 2
        P.dma("pool", x2_sb[:, :], x_local[c * 128:(c + 1) * 128, :], PQA,
              waits=[("CC", e_cc[g2]), (PQA, e_x2w[c - 1] if c else 0),
                     ("DVE", P.n("DVE"))])
        eg1 = P.n(PQA)
        P.dma("pool", x2_sb[:, :], rs_out[c * 128:(c + 1) * 128, :], PQA,
              waits=[(PQA, eg1)], accum_op=ALU.add)
        e_l = P.n(PQA)
        P.dma("pool", x2_buf[c * 128:(c + 1) * 128, :], x2_sb[:, :], PQA,
              waits=[(PQA, e_l)])
        e_x2w[c] = P.n(PQA)
        for i in range(4):
            P.op("dve", lambda e, o=bn6[:, i * 6:(i + 1) * 6],
                 s=x2_sb[:, i * 512:(i + 1) * 512]: e.bn_stats(o, s),
                 waits=[(PQA, e_l)] if i == 0 else [], sig="DVE")
        eb = P.n("DVE")
        P.op("dve", lambda e: e.bn_aggr(mv[:, :], bn6[:, :]), waits=[("DVE", eb)],
             sig="DVE")
        ea = P.n("DVE")
        P.tt(lnt[0][:, :], mv[:, 1:2], eps_sb[:, :], ALU.add,
             waits=[("DVE", ea), ("ACT", P.n("ACT"))])
        ev = P.n("DVE")
        el = P.act(lnt[1][:, :], lnt[0][:, :], AF.Ln, waits=[("DVE", ev)])
        er = P.act(rstd[:, :], lnt[1][:, :], AF.Exp, scale=-0.5, waits=[("ACT", el)])
        P.ts(ln2_sb[:, :], x2_sb[:, :], mv[:, 0:1], rstd[:, :], ALU.subtract, ALU.mult,
             waits=[("ACT", er), (PQB, e_ln2w[c - 1] if c else 0)])
        e_ts = P.n("DVE")
        P.dma("pool", ln2_tm[c * 128:(c + 1) * 128, :], ln2_sb[:, :], PQB,
              waits=[("DVE", e_ts)])
        e_ln2w[c] = P.n(PQB)

    def emit_attn_tail():
        nonlocal nbc_evac
        if pending_tail[0] is None:
            return
        h_, csl_, nsl_ = pending_tail[0]
        pending_tail[0] = None
        e_bc = P.mm(psBC[:, :], ones1f[:, :], rcp_sb[:, :], start=True, stop=True,
                    waits=[("DVE", n_free2[nsl_]), ("DVE", nbc_evac)])
        P.op("dve", lambda e: e.tensor_copy(nbc_sb[:, :], psBC[:, :]),
             waits=[("PE", e_bc)], sig="DVE")
        nbc_evac = P.n("DVE")
        P.tt(ctxs[:, h_ * 512:(h_ + 1) * 512], psCs[csl_][:, :], nbc_sb[:, :],
             ALU.mult, waits=[("DVE", nbc_evac), ("PE", ctxs_free)])
        ctx_free2[csl_] = P.n("DVE")

    for g in range(NG):
        ext = 4 * (g + 1)
        for hi in range(HPC):
            hh_idx = g * HPC + hi
            qsl = hh_idx % 2
            csl = hh_idx % 2
            nsl = hi % 2
            ssl_q, val_q = e_kqw[(hi, g, True)]
            P.dma("sp", q_sb[qsl][:, :], q_buf[hi, :, g * 512:(g + 1) * 512], QL[qsl],
                  waits=[(KSQ[ssl_q], val_q), ("PE", q_free[qsl])])
            e_q = P.n(QL[qsl])
            kw = [(KSQ[i], max((v for (sl_, v) in
                                (e_kqw[(hi, tb, False)] for tb in range(g + 1))
                                if sl_ == i), default=0)) for i in range(4)]
            P.dma("sp", k_sb[qsl][:, 0:ext * 128], k_buf[hi, :, 0:ext * 128], KL[qsl],
                  waits=kw + [("PE", k_free[qsl])])
            e_k = P.n(KL[qsl])

            def sc_step(i):
                bs = i % 2
                w = [(QL[qsl], e_q), (KL[qsl], e_k), ("ACT", scoresE[bs])]
                if g == 0 and hi == 0 and i == 0:
                    w.append(("ACT", e_vdone))
                e_s = P.mm(psS[bs][:, :], k_sb[qsl][:, i * 128:(i + 1) * 128],
                           q_sb[qsl][:, :], start=True, stop=True, waits=w)
                pe_sl = i % 4
                scoresE[bs] = P.act(probsE[pe_sl][:, :], psS[bs][:, :], AF.Exp,
                                    bias=csh_sb[:, 0:1], scale=ISQ,
                                    waits=[("PE", e_s), probsE_free[pe_sl],
                                           ("GP", e_const)])
                e_e = scoresE[bs]
                diag_m = i - 4 * g
                if diag_m >= 0:
                    pm_sl = diag_m % 2
                    off = (3 - diag_m) * 128
                    P.tt(probsM[pm_sl][:, :], probsE[pe_sl][:, :],
                         diag1[:, off:off + 512], ALU.mult,
                         waits=[("ACT", e_e), ("GP", e_diag),
                                ("PE", probsM_free[pm_sl])])
                    e_tt = P.n("DVE")
                    return (i, pe_sl, pm_sl, probsM[pm_sl], [("DVE", e_tt)], e_tt)
                return (i, pe_sl, None, probsE[pe_sl], [("ACT", e_e)], None)

            def npv_step(st):
                i, pe_sl, pm_sl, pt, w_pe, e_tt = st
                w = list(w_pe)
                if i == 0:
                    w += [("DVE", n_free2[nsl]), ("DVE", ctx_free2[csl])]
                    if nsl == 1:
                        w.append(("ACT", psO_free[0]))
                P.mm(psNs[nsl], ones128[:, :], pt[:, :], start=(i == 0),
                     stop=(i == ext - 1), waits=w)
                lp = P.mm(psCs[csl][:, :],
                          vstore[:, i * 1024 + hi * 128: i * 1024 + hi * 128 + 128],
                          pt[:, :], start=(i == 0), stop=(i == ext - 1))
                probsE_free[pe_sl] = (("DVE", e_tt) if pm_sl is not None
                                      else ("PE", lp))
                if pm_sl is not None:
                    probsM_free[pm_sl] = lp
                return lp

            prev = sc_step(0)
            emit_attn_tail()
            last_pv = 0
            for i in range(1, ext):
                st = sc_step(i)
                last_pv = npv_step(prev)
                prev = st
            last_pv = npv_step(prev)
            q_free[qsl] = last_pv
            k_free[qsl] = last_pv
            P.op("dve", lambda e, nb=psNs[nsl]: e.reciprocal(rcp_sb[:, :], nb),
                 waits=[("PE", last_pv), ("DVE", P.n("DVE"))], sig="DVE")
            n_free2[nsl] = P.n("DVE")
            pending_tail[0] = (hi, csl, nsl)
        emit_attn_tail()
        e_ctx_all = P.n("DVE")

        for tc in range(4):
            for do in range(4):
                bank = (tc * 4 + do) % 2
                w = [("DVE", e_ctx_all), (WOQ, e_wout), ("ACT", e_bouth),
                     ("ACT", psO_free[bank])]
                if tc == 0 and do == 0:
                    w.append(("DVE", n_free2[1]))
                for hh in range(HPC):
                    P.mm(psO[bank][:, :],
                         ctxs[:, hh * 512 + tc * 128: hh * 512 + tc * 128 + 128],
                         wout_sb[:, hh * 2048 + do * 512: hh * 2048 + do * 512 + 512],
                         start=(hh == 0), stop=False, waits=w if hh == 0 else [])
                e_o = P.mm(psO[bank][:, :], ones1f[:, :],
                           b_out_h[0:1, do * 512:(do + 1) * 512], start=False, stop=True)
                osl = (tc * 4 + do) % 2
                psO_free[bank] = P.act(ostage[osl][:, :], psO[bank][:, :], AF.Copy,
                                       bias=0.0,
                                       waits=[("PE", e_o), (OSQ[osl], o_free[osl])])
                P.dma("sp", rs_in[g * 512 + tc * 128: g * 512 + tc * 128 + 128,
                                  do * 512:(do + 1) * 512],
                      ostage[osl][:, :], OSQ[osl], waits=[("ACT", psO_free[bank])])
                o_free[osl] = P.n(OSQ[osl])
        ctxs_free = P.n("PE")
        e_rsw0 = P.n(OSQ[0])
        e_rsw1 = P.n(OSQ[1])

        P.op("pool", lambda gp, a=rs_in[g * 512:(g + 1) * 512, :].opt(),
             b=rs_out[g * 256:(g + 1) * 256, :].opt():
             gp.collective_compute("ReduceScatter", ALU.add,
                                   replica_groups=[[0, 1], [2, 3], [4, 5], [6, 7]],
                                   ins=[a], outs=[b]),
             waits=[(OSQ[0], e_rsw0), (OSQ[1], e_rsw1)], sig="CC")
        e_cc[g] = P.n("CC")
        while p3a_done < 2 * (g - 1):
            p3a_chunk(p3a_done)
            p3a_done += 1
    while p3a_done < 16:
        p3a_chunk(p3a_done)
        p3a_done += 1

    e_p2pe = P.n("PE")
    e_p2act = P.n("ACT")
    e_p2dve = P.n("DVE")
    s2.close()
    s12.close()

    s3 = ExitStack()
    sb3 = lambda name, shape, dt: s3.enter_context(nc.sbuf_tensor(name, shape, dt))
    ps3 = [s3.enter_context(nc.psum_tensor(f"ps3_{i}", [128, 512], F32)) for i in range(6)]
    xfm2 = sb3("xfm2", [128, 16 * 512], BF16)
    w1buf = [sb3(f"w1buf{i}", [128, 2240], BF16) for i in range(3)]
    w2buf = [sb3(f"w2buf{i}", [128, 1024], BF16) for i in range(3)]
    h_sb = [sb3("h0_sb", [128, 64 * 512], BF16), sb3("h1_sb", [128, 64 * 512], BF16)]
    tmpb1 = sb3("tmpb1", [128, 512], F32)
    gin = sb3("gin", [128, 512], F32)
    x2t = [sb3(f"x2t{i}", [128, 512], F32) for i in range(2)]
    ot = [sb3(f"ot{i}", [128, 512], F32) for i in range(2)]
    ot2 = [sb3(f"ot2_{i}", [128, 512], F32) for i in range(2)]
    b1_sb = sb3("b1_sb", [128, 64], F32)
    b2_sb = sb3("b2_sb", [1, 2048], F32)
    br_sb = sb3("br_sb", [1, 8], F32)
    oh_fm = [sb3(f"oh_fm{k}", [8, 512], BF16) for k in range(2)]
    m1_sb = [sb3(f"m1_sb{k}", [64, 512], BF16) for k in range(2)]
    z1_sb = [sb3(f"z1_sb{k}", [64, 512], BF16) for k in range(2)]
    z2_sb = [sb3(f"z2_sb{k}", [64, 512], BF16) for k in range(2)]

    e_join = [("PE", e_p2pe), ("ACT", e_p2act), ("DVE", e_p2dve)]
    SM3 = P.qsem("sm3")
    P.dma("sp", b1_sb[:, :], b1h, SM3, waits=[("ACT", e_p2act), ("DVE", e_p2dve),
                                              ("PE", e_p2pe)])
    P.dma("sp", b2_sb[:, :], b2h, SM3)
    P.dma("sp", br_sb[:, :], brh, SM3)
    e_small3 = P.n(SM3)

    TR2 = P.qsem("tr2")
    W1S = [P.qsem(f"w1s{i}") for i in range(3)]
    W1Q = ["sp", "pool", "act"]
    W2S = [P.qsem(f"w2s{i}") for i in range(3)]
    W2Q = ["act", "sp", "pool"]
    X2TQ = [P.qsem("x2tq0"), P.qsem("x2tq1")]
    OUTQ = [P.qsem("outq0"), P.qsem("outq1")]

    xfm2_free = 0
    w1_free = [0] * 3
    w2_free = [0] * 3
    e_tail_dve = 0
    x2t_free = [0, 0]
    ot_free = [0, 0]
    ot2_free = [0, 0]

    for t in range(4):
        for dc in range(16):
            w = []
            if dc == 0:
                w = [(PQB, e_ln2w[4 * t + 3]), ("PE", xfm2_free)]
                if t == 0:
                    w += e_join
            P.dmat(xfm2[:, dc * 512:(dc + 1) * 512],
                   ln2_tm[t * 512:(t + 1) * 512, dc * 128:(dc + 1) * 128], TR2,
                   waits=w)
        e_xf = P.n(TR2)

        sps = ExitStack()
        psB16 = sps.enter_context(nc.psum_tensor(f"psB16_{t}", [8, 128], BF16))
        e_m1 = [0, 0]
        e_exp_prev = 0
        for tc in range(4):
            w = [(TR2, e_xf), (SMQ, e_smallw), ("DVE", e_tail_dve),
                 ("ACT", e_exp_prev)]
            if t == 0 and tc == 0:
                w += e_join
            for dc in range(16):
                P.mm(ps3[4][0:128, 0:8],
                     xfm2[:, dc * 512 + tc * 128: dc * 512 + tc * 128 + 128],
                     wrt_sb[:, dc * 8:(dc + 1) * 8], start=(dc == 0), stop=False,
                     waits=w if dc == 0 else [])
            e_r = P.mm(ps3[4][0:128, 0:8], ones1f[:, :], br_sb[:, :], start=False,
                       stop=True, waits=[(SM3, e_small3)])
            P.op("dve", lambda e: e.reduce_max(r_sc[0][:, :], ps3[4][0:128, 0:8],
                                               axis=AX.X, negate=True),
                 waits=[("PE", e_r), ("DVE", P.n("DVE"))], sig="DVE")
            e1 = P.n("DVE")
            e2 = P.act(probs_r[:, :], ps3[4][0:128, 0:8], AF.Exp, bias=r_sc[0][:, :],
                       waits=[("DVE", e1), ("ACT", P.n("ACT"))])
            e_exp_prev = e2
            P.op("dve", lambda e: e.reduce_sum(r_sc[1][:, :], probs_r[:, :], axis=AX.X),
                 waits=[("ACT", e2)], sig="DVE")
            e3 = P.n("DVE")
            P.op("dve", lambda e: e.reciprocal(r_sc[2][:, :], r_sc[1][:, :]),
                 waits=[("DVE", e3)], sig="DVE")
            e4 = P.n("DVE")
            P.ts(probs_n[:, :], probs_r[:, :], r_sc[2][:, :], None, ALU.mult, ALU.bypass,
                 waits=[("DVE", e4)])
            e5 = P.n("DVE")
            P.op("dve", lambda e, o=w0_st[:, tc:tc + 1]:
                 e.reduce_max(o, probs_n[:, :], axis=AX.X), waits=[("DVE", e5)],
                 sig="DVE")
            e6 = P.n("DVE")
            P.ts(oh_tm[0][:, :], probs_n[:, :], w0_st[:, tc:tc + 1], None,
                 ALU.is_equal, ALU.bypass, waits=[("DVE", e6), ("PE", P.n("PE"))])
            e7 = P.n("DVE")
            P.tt(probs_r[:, :], probs_n[:, :], oh_tm[0][:, :], ALU.mult,
                 waits=[("DVE", e7)])
            e7b = P.n("DVE")
            P.tt(probs_m[:, :], probs_n[:, :], probs_r[:, :], ALU.subtract,
                 waits=[("DVE", e7b)])
            e8 = P.n("DVE")
            P.op("dve", lambda e, o=w1_st[:, tc:tc + 1]:
                 e.reduce_max(o, probs_m[:, :], axis=AX.X), waits=[("DVE", e8)],
                 sig="DVE")
            e9 = P.n("DVE")
            P.ts(oh_tm[1][:, :], probs_m[:, :], w1_st[:, tc:tc + 1], None,
                 ALU.is_equal, ALU.bypass, waits=[("DVE", e9), ("PE", P.n("PE"))])
            e10 = P.n("DVE")
            for k in range(2):
                ep = P.transp(psB16[0:8, 0:128], oh_tm[k][:, :], ident[:, :],
                              waits=[("DVE", e10), ("GP", e_ident),
                                     ("ACT", P.n("ACT")), ("DVE", e_tail_dve)])
                P.act(oh_fm[k][:, tc * 128:(tc + 1) * 128], psB16[0:8, 0:128], AF.Copy,
                      bias=0.0, waits=[("PE", ep)])
        e_oh = P.n("ACT")
        for k in range(2):
            em = P.mm(ps3[5][0:64, :], rep2[:, :], oh_fm[k][:, :], start=True, stop=True,
                      waits=[("ACT", e_oh), ("GP", e_rep2), ("DVE", P.n("DVE"))])
            P.op("dve", lambda e, k=k: e.tensor_copy(m1_sb[k][:, :], ps3[5][0:64, :]),
                 waits=[("PE", em)], sig="DVE")
            e_m1[k] = P.n("DVE")
        sps.close()
        psE = ExitStack()
        ps6 = psE.enter_context(nc.psum_tensor(f"ps3_6_{t}", [128, 512], F32))
        ps7 = psE.enter_context(nc.psum_tensor(f"ps3_7_{t}", [128, 512], F32))
        ps8 = [ps3[0], ps3[1], ps3[2], ps3[3], ps3[4], ps3[5], ps6, ps7]

        for dc in range(16):
            P.mm(ps3[5][0:64, :], a1t_sb[:, dc * 64:(dc + 1) * 64],
                 xfm2[:, dc * 512:(dc + 1) * 512], start=(dc == 0), stop=(dc == 15),
                 waits=[(SMQ, e_smallw), ("DVE", e_m1[1])] if dc == 0 else [])
        e_u1 = P.n("PE")
        ez1 = []
        for k in range(2):
            P.tt(z1_sb[k][:, :], ps3[5][0:64, :], m1_sb[k][:, :], ALU.mult,
                 waits=[("PE", e_u1), ("DVE", e_m1[k])])
            ez1.append(P.n("DVE"))

        tmp_free = 0
        gin_free = 0
        e_b1evac = {}
        e_gelu = {}
        e_ttg = {}
        e_u2mm = [0, 0]
        for f in range(64):
            wsl = f % 3
            P.dma(W1Q[wsl], w1buf[wsl][:, :], w1c[f], W1S[wsl],
                  waits=[(CS[0], e_wc[0]), (CS[1], e_wc[1]), ("PE", w1_free[wsl])] +
                        (e_join if (t == 0 and f < 3) else []))
            e_wl = P.n(W1S[wsl])
            bb = f % 2
            for dc in range(16):
                w = []
                if dc == 0:
                    w = [(W1S[wsl], e_wl), (TR2, e_xf), ("ACT", e_b1evac.get(f - 2, 0))]
                    if f == 0:
                        w.append(("DVE", e_m1[1]))
                P.mm(ps3[bb][:, :], w1buf[wsl][:, dc * 128:(dc + 1) * 128],
                     xfm2[:, dc * 512:(dc + 1) * 512], start=(dc == 0), stop=(dc == 15),
                     waits=w)
            e_b1 = P.n("PE")
            e_b1evac[f] = P.act(tmpb1[:, :], ps3[bb][:, :], AF.Identity,
                                bias=b1_sb[:, f:f + 1],
                                waits=[("PE", e_b1), ("DVE", tmp_free),
                                       (SM3, e_small3)])
            for k in range(2):
                bz = 2 + k
                e_zb = P.mm(ps3[bz][:, :], w1buf[wsl][0:64, 2048:2176], z1_sb[k][:, :],
                            start=True, stop=True,
                            waits=[("DVE", ez1[k]), ("DVE", e_ttg.get((f - 1, k), 0))])
                P.tt(gin[:, :], ps3[bz][:, :], tmpb1[:, :], ALU.add,
                     waits=[("PE", e_zb), ("ACT", e_b1evac[f]), ("ACT", gin_free)])
                e_ttg[(f, k)] = P.n("DVE")
                if k == 1:
                    tmp_free = e_ttg[(f, k)]
                e_gelu[(f, k)] = P.act(h_sb[k][:, f * 512:(f + 1) * 512], gin[:, :],
                                       AF.Gelu_apprx_tanh,
                                       waits=[("DVE", e_ttg[(f, k)])])
                gin_free = e_gelu[(f, k)]
                e_u2mm[k] = P.mm(ps3[4 + k][0:64, :], w1buf[wsl][:, 2176:2240],
                                 h_sb[k][:, f * 512:(f + 1) * 512],
                                 start=(f == 0), stop=(f == 63),
                                 waits=[("ACT", e_gelu[(f, k)])])
            w1_free[wsl] = P.n("PE")
        ez2 = []
        for k in range(2):
            P.tt(z2_sb[k][:, :], ps3[4 + k][0:64, :], m1_sb[k][:, :], ALU.mult,
                 waits=[("PE", e_u2mm[k]), ("DVE", e_m1[k])])
            ez2.append(P.n("DVE"))
        e_b1last = e_b1evac[63]
        e_ttglast = e_ttg[(63, 1)]

        e_stt2 = {}
        for do in range(4):
            for tc in range(4):
                for k in range(2):
                    bank = ps8[k * 4 + tc]
                    w = [(SM3, e_small3), ("DVE", ez2[k])]
                    if do == 0:
                        w += [("ACT", e_b1last), ("DVE", e_ttglast), ("DVE", ez2[1]),
                              ("DVE", e_tail_dve)]
                    else:
                        w.append(("DVE", e_stt2[(do - 1, tc)]))
                    P.mm(bank[:, :], ones1f[:, :], b2_sb[0:1, do * 512:(do + 1) * 512],
                         start=True, stop=False, waits=w)
                    P.mm(bank[:, :], z2_sb[k][:, tc * 128:(tc + 1) * 128],
                         b2rf_sb[0:64, do * 512:(do + 1) * 512], start=False, stop=False)
            for fq in range(32):
                wsl = fq % 3
                P.dma(W2Q[wsl], w2buf[wsl][:, :], w2cs[do][fq], W2S[wsl],
                      waits=[(CS[0], e_wc[0]), (CS[1], e_wc[1]),
                             ("PE", w2_free[wsl])] +
                            (e_join if (t == 0 and do == 0 and fq < 3) else []))
                e_w2 = P.n(W2S[wsl])
                for fi in range(2):
                    f = 2 * fq + fi
                    for k in range(2):
                        for tc in range(4):
                            w = []
                            if fi == 0 and k == 0 and tc == 0:
                                w = [(W2S[wsl], e_w2), ("ACT", e_gelu[(63, 1)])]
                            P.mm(ps8[k * 4 + tc][:, :],
                                 h_sb[k][:, f * 512 + tc * 128: f * 512 + tc * 128 + 128],
                                 w2buf[wsl][:, fi * 512:(fi + 1) * 512],
                                 start=False, stop=(f == 63), waits=w)
                w2_free[wsl] = P.n("PE")
            e_stop = P.n("PE")

            for tc in range(4):
                xsl = tc % 2
                P.dma("sp", x2t[xsl][:, :],
                      x2_buf[(t * 4 + tc) * 128:(t * 4 + tc) * 128 + 128,
                             do * 512:(do + 1) * 512], X2TQ[xsl],
                      waits=[(PQA, e_x2w[t * 4 + tc]), ("DVE", x2t_free[xsl])])
                e_xl = P.n(X2TQ[xsl])
                osl = (do * 4 + tc) % 2
                P.stt(ot[osl][:, :], ps8[tc][:, :], w0_st[:, tc:tc + 1],
                      x2t[xsl][:, :], ALU.mult, ALU.add,
                      waits=[("PE", e_stop), (X2TQ[xsl], e_xl), ("DVE", ot_free[osl])])
                x2t_free[xsl] = P.n("DVE")
                P.stt(ot2[osl][:, :], ps8[4 + tc][:, :], w1_st[:, tc:tc + 1],
                      ot[osl][:, :], ALU.mult, ALU.add,
                      waits=[("DVE", x2t_free[xsl]), (OUTQ[osl], ot2_free[osl])])
                e_stt2[(do, tc)] = P.n("DVE")
                ot_free[osl] = e_stt2[(do, tc)]
                P.dma("sp", out_t[(t * 4 + tc) * 128:(t * 4 + tc) * 128 + 128,
                                  do * 512:(do + 1) * 512],
                      ot2[osl][:, :], OUTQ[osl], waits=[("DVE", e_stt2[(do, tc)])])
                ot2_free[osl] = P.n(OUTQ[osl])
        e_tail_dve = P.n("DVE")
        xfm2_free = P.n("PE")
        psE.close()

    if taps:
        TAPQ = P.qsem("tapq")
        allq = [(name, P.n(name)) for name in P.sems if name not in ("CC",)]
        for tname in taps:
            src = tap_src[tname][0]
            P.dma("pool", tap_aps[tname], src[tuple(slice(None) for _ in src.shape)],
                  TAPQ, waits=allq + [("CC", P.n("CC"))])
            allq = []
        P.op("sp", lambda e: e.nop(),
             waits=[(TAPQ, P.n(TAPQ)), (OUTQ[0], P.n(OUTQ[0])),
                    (OUTQ[1], P.n(OUTQ[1]))], sig=None)
    else:
        P.op("sp", lambda e: e.nop(), waits=[(OUTQ[0], P.n(OUTQ[0])),
                                             (OUTQ[1], P.n(OUTQ[1]))], sig=None)

    s3.close()
    with nc.Block() as block:
        P.play(block)
    es.close()
    return nc


def _local_rows(r):
    return np.concatenate([np.arange(g * 512 + r * 256, g * 512 + r * 256 + 256)
                           for g in range(8)])


def _prep_core_inputs(c, I):
    b, r = c // 2, c % 2
    A = lambda x: np.ascontiguousarray(x, dtype=np.float32)
    X = np.asarray(I["L0_attn_x"], dtype=np.float32)
    w_qkv = np.asarray(I["L0_attn_w_qkv"], dtype=np.float32)
    heads = range(r * 8, r * 8 + 8)
    qk_cols = np.concatenate([np.r_[h * 384:h * 384 + 128, h * 384 + 128:h * 384 + 256]
                              for h in heads])
    v_cols = np.concatenate([np.r_[h * 384 + 256:h * 384 + 384] for h in heads])
    w_qk = w_qkv[:, qk_cols]
    d = {}
    d["x_batch"] = A(X[b])
    d["x_local"] = A(X[b][_local_rows(r)])
    d["wqkb"] = A(w_qk.reshape(16, 128, 16, 128).transpose(2, 1, 0, 3).reshape(16, 128, 2048))
    b_qkv = np.asarray(I["L0_attn_b_qkv"], dtype=np.float32)
    d["b_qk"] = A(b_qkv[qk_cols].reshape(16, 128).T)
    d["wvb"] = A(w_qkv[:, v_cols].reshape(16, 128, 1024))
    d["b_v"] = A(b_qkv[v_cols].reshape(1, 1024))
    w_out = np.asarray(I["L0_attn_w_out"], dtype=np.float32)
    d["woutb"] = A(w_out[r * 1024:(r + 1) * 1024].reshape(8, 128, 2048))
    d["b_out"] = A(np.asarray(I["L0_attn_b_out"], np.float32).reshape(1, 2048))
    W1 = np.asarray(I["L0_ffn_W1"], dtype=np.float32)
    w1tb = np.zeros((64, 128, 2240), np.float32)
    w1tb[:, :, 0:2048] = W1.reshape(64, 128, 16, 128).transpose(0, 3, 2, 1).reshape(64, 128, 2048)
    B1rf = np.asarray(I["L0_ffn_B1"], np.float32).transpose(0, 2, 1).reshape(64, 8192)
    w1tb[:, 0:64, 2048:2176] = B1rf.reshape(64, 64, 128).transpose(1, 0, 2)
    A2rf = np.asarray(I["L0_ffn_A2"], np.float32).reshape(64, 8192)
    w1tb[:, :, 2176:2240] = A2rf.T.reshape(64, 128, 64)
    d["w1tb"] = A(w1tb)
    d["b1h"] = A(np.asarray(I["L0_ffn_b1"], np.float32).reshape(64, 128).T)
    W2T = np.asarray(I["L0_ffn_W2"], np.float32).T
    d["w2tb"] = A(W2T.reshape(64, 128, 4, 512).transpose(2, 0, 1, 3))
    d["b2h"] = A(np.asarray(I["L0_ffn_b2"], np.float32).reshape(1, 2048))
    A1rf = np.asarray(I["L0_ffn_A1"], np.float32).reshape(64, 2048)
    d["a1tb"] = A(A1rf.T.reshape(16, 128, 64).transpose(1, 0, 2).reshape(128, 1024))
    b2rf = np.zeros((128, 2048), np.float32)
    b2rf[0:64] = np.asarray(I["L0_ffn_B2"], np.float32).transpose(0, 2, 1).reshape(64, 2048)
    d["b2rfh"] = A(b2rf)
    d["wrtb"] = A(np.asarray(I["L0_ffn_W_r"], np.float32).T.reshape(16, 128, 8)
                  .transpose(1, 0, 2).reshape(128, 128))
    d["brh"] = A(np.asarray(I["L0_ffn_b_r"], np.float32).reshape(1, 8))
    return d


def kernel(**inputs):
    nc = build_program()
    in_maps = [_prep_core_inputs(c, inputs) for c in range(NCORES)]
    res = run_bass_kernel_spmd(nc, in_maps, core_ids=list(range(NCORES)))
    out = np.empty((B, S, D), np.float32)
    for c in range(NCORES):
        b, r = c // 2, c % 2
        out[b][_local_rows(r)] = res.results[c]["out"]
    return out


# version 03 (final)
# speedup vs baseline: 1.1880x; 1.1515x over previous
import numpy as np
import concourse.bass as bass
import concourse.mybir as mybir
from concourse.bass_utils import run_bass_kernel_spmd
from contextlib import ExitStack

F32 = mybir.dt.float32
BF16 = mybir.dt.bfloat16
AF = mybir.ActivationFunctionType
ALU = mybir.AluOpType
AX = mybir.AxisListType

B, S, D = 4, 4096, 2048
HPC = 8
NT = 2048
NG = 8
CSHIFT = 14.0
ISQ = float(1.0 / np.sqrt(128.0))
NCORES = 8

ENGS = ("pe", "act", "dve", "pool", "sp")
SEMS = ("PE", "ACT", "DVE", "GP", "CC")


class Plan:
    def __init__(self, nc, es):
        self.nc = nc
        self.es = es
        self.ops = {e: [] for e in ENGS}
        self.sems = {}
        for name in SEMS:
            self.sems[name] = [es.enter_context(nc.semaphore(name.lower())), 0]

    def qsem(self, name):
        assert name not in self.sems
        self.sems[name] = [self.es.enter_context(self.nc.semaphore(name)), 0]
        return name

    def n(self, s):
        return self.sems[s][1]

    def op(self, eng, fn, waits=(), sig=None, amt=1):
        ws = [(s, v) for (s, v) in waits if v]
        for s, v in ws:
            assert v <= self.sems[s][1], f"wait {s}>={v}, planned {self.sems[s][1]}"
        rec = None
        if sig is not None:
            self.sems[sig][1] += amt
            rec = (sig, amt)
        self.ops[eng].append((ws, fn, rec))
        return self.sems[sig][1] if sig else None

    def play(self, block):
        sems = self.sems

        def body_for(key):
            ops = self.ops[key]

            def body(eng):
                for ws, fn, rec in ops:
                    for s, v in ws:
                        eng.wait_ge(sems[s][0], v)
                    inst = fn(eng)
                    if rec is not None:
                        inst.then_inc(sems[rec[0]][0], rec[1])
            return body
        block.tensor(body_for("pe"))
        block.scalar(body_for("act"))
        block.vector(body_for("dve"))
        block.gpsimd(body_for("pool"))
        block.sync(body_for("sp"))

    def mm(self, out, lhsT, rhs, start, stop, waits=()):
        return self.op("pe", lambda t, o=out, l=lhsT, r=rhs, a=start, b=stop:
                       t.matmul(o, lhsT=l, rhs=r, start=a, stop=b), waits, "PE")

    def transp(self, out, in_, ident, waits=()):
        return self.op("pe", lambda t, o=out, i=in_, d=ident:
                       t.transpose(o, i, d), waits, "PE")

    def dma(self, eng, out, in_, q, waits=(), **kw):
        return self.op(eng, lambda e, o=out, i=in_, k=kw: e.dma_start(out=o, in_=i, **k),
                       waits, q, 16)

    def dmat(self, out, in_, q, waits=()):
        return self.op("act", lambda e, o=out, i=in_: e.dma_start_transpose(o, i),
                       waits, q, 16)

    def act(self, out, in_, func, bias=0.0, scale=1.0, waits=()):
        return self.op("act", lambda e, o=out, i=in_, fn=func, b=bias, sc=scale:
                       e.activation(o, i, fn, bias=b, scale=sc), waits, "ACT")

    def tt(self, out, in0, in1, opr, waits=()):
        return self.op("dve", lambda e, o=out, a=in0, b=in1, p=opr:
                       e.tensor_tensor(o, a, b, op=p), waits, "DVE")

    def ts(self, out, in0, s1, s2, op0, op1, waits=()):
        return self.op("dve", lambda e, o=out, a=in0, x=s1, y=s2, p0=op0, p1=op1:
                       e.tensor_scalar(o, a, x, y, op0=p0, op1=p1), waits, "DVE")

    def stt(self, out, in0, scalar, in1, op0, op1, waits=()):
        return self.op("dve", lambda e, o=out, a=in0, s=scalar, b=in1, p0=op0, p1=op1:
                       e.scalar_tensor_tensor(o, a, s, b, op0=p0, op1=p1), waits, "DVE")


def build_program(taps=(), phases=3):
    nc = bass.Bass("TRN2", target_bir_lowering=False, debug=False)
    es = ExitStack()

    def din(name, shape, dt=F32):
        return nc.dram_tensor(name, shape, dt, kind="ExternalInput").ap()

    x_batch = din("x_batch", [S, D])
    x_local = din("x_local", [NT, D])
    wqkb = din("wqkb", [16, 128, 2048])
    b_qk = din("b_qk", [128, 16])
    wvb = din("wvb", [16, 128, 1024])
    b_v = din("b_v", [1, 1024])
    woutb = din("woutb", [8, 128, 2048])
    b_out = din("b_out", [1, 2048])
    w1tb = din("w1tb", [64, 128, 2240])
    b1h = din("b1h", [128, 64])
    w2tb = din("w2tb", [4, 64, 128, 512])
    b2h = din("b2h", [1, 2048])
    a1tb = din("a1tb", [128, 1024])
    b2rfh = din("b2rfh", [128, 2048])
    wrtb = din("wrtb", [128, 128])
    brh = din("brh", [1, 8])

    out_t = nc.dram_tensor("out", [NT, D], F32, kind="ExternalOutput").ap()

    ln1_tm = nc.dram_tensor("ln1_tm", [S, D], BF16).ap()
    q_buf = nc.dram_tensor("q_buf", [HPC, 128, S], BF16).ap()
    k_buf = nc.dram_tensor("k_buf", [HPC, 128, S], BF16).ap()
    rs_in = nc.dram_tensor("rs_in", [S, D], BF16).ap()
    rs_out = nc.dram_tensor("rs_out", [NT, D], BF16).ap()
    ln2_tm = nc.dram_tensor("ln2_tm", [NT, D], BF16).ap()
    x2_buf = nc.dram_tensor("x2_buf", [NT, D], F32).ap()
    wqkc = nc.dram_tensor("wqkc", [16, 128, 2048], BF16).ap()
    wvc = nc.dram_tensor("wvc", [16, 128, 1024], BF16).ap()
    w1c = nc.dram_tensor("w1c", [64, 128, 2240], BF16).ap()
    w2cs = [nc.dram_tensor(f"w2cs{do}", [32, 128, 1024], BF16).ap() for do in range(4)]
    woutc = nc.dram_tensor("woutc", [8, 128, 2048], BF16).ap()

    tap_src = {"t_ln1": (ln1_tm, [S, D]), "t_q": (q_buf, [HPC, 128, S]),
               "t_k": (k_buf, [HPC, 128, S]), "t_rs": (rs_out, [NT, D]),
               "t_x2": (x2_buf, [NT, D]), "t_ln2": (ln2_tm, [NT, D]),
               "t_rsin": (rs_in, [S, D])}
    tap_aps = {}
    for tname in taps:
        tap_aps[tname] = nc.dram_tensor(tname, tap_src[tname][1], F32,
                                        kind="ExternalOutput").ap()

    P = Plan(nc, es)
    sb = lambda name, shape, dt: es.enter_context(nc.sbuf_tensor(name, shape, dt))

    ident = sb("ident", [128, 128], BF16)
    rep2 = sb("rep2", [8, 64], BF16)
    ones128 = sb("ones128", [128, 1], BF16)
    ones1f = sb("ones1f", [1, 128], F32)
    wrt_sb = sb("wrt_sb", [128, 128], BF16)
    a1t_sb = sb("a1t_sb", [128, 1024], BF16)
    b2rf_sb = sb("b2rf_sb", [128, 2048], BF16)
    w0_st = sb("w0_st", [128, 4], F32)
    w1_st = sb("w1_st", [128, 4], F32)
    r_sc = [sb(f"r_sc{i}", [128, 1], F32) for i in range(3)]
    probs_r = sb("probs_r", [128, 8], F32)
    probs_n = sb("probs_n", [128, 8], F32)
    probs_m = sb("probs_m", [128, 8], F32)
    oh_tm = [sb(f"oh_tm{k}", [128, 8], BF16) for k in range(2)]
    eps_sb = sb("eps_sb", [128, 1], F32)
    csh_sb = sb("csh_sb", [128, 1], F32)
    lnt = [sb(f"lnt{i}", [128, 1], F32) for i in range(2)]
    bn6 = sb("bn6", [128, 24], F32)
    mv = sb("mv", [128, 2], F32)
    rstd = sb("rstd", [128, 1], F32)

    def pl(fn, waits=()):
        return P.op("pool", fn, waits, "GP")

    e0 = pl(lambda g: g.memset(ident[:, :], 1.0))
    e_ident = pl(lambda g: g.affine_select(ident[:, :], ident[:, :], pattern=[[1, 128]],
                                           compare_op=ALU.is_equal, fill=0.0, base=0,
                                           channel_multiplier=-1), [("GP", e0)])
    e0 = pl(lambda g: g.memset(rep2[:, :], 2.0))
    e0 = pl(lambda g: g.affine_select(rep2[:, :], rep2[:, :], pattern=[[1, 64]],
                                      compare_op=ALU.is_ge, fill=0.0, base=0,
                                      channel_multiplier=-8), [("GP", e0)])
    e_rep2 = pl(lambda g: g.affine_select(rep2[:, :], rep2[:, :], pattern=[[-1, 64]],
                                          compare_op=ALU.is_ge, fill=0.0, base=7,
                                          channel_multiplier=8), [("GP", e0)])
    pl(lambda g: g.memset(ones128[:, :], 1.0))
    pl(lambda g: g.memset(ones1f[:, :], 1.0))
    pl(lambda g: g.memset(eps_sb[:, :], 1e-5))
    e_const = pl(lambda g: g.memset(csh_sb[:, :], -CSHIFT))

    SMQ = P.qsem("smq")
    P.dma("pool", wrt_sb[:, :], wrtb, SMQ, waits=[("GP", e_const)])
    P.dma("pool", a1t_sb[:, :], a1tb, SMQ)
    P.dma("pool", b2rf_sb[:, :], b2rfh, SMQ)
    e_smallw = P.n(SMQ)

    s12 = ExitStack()
    vstore = s12.enter_context(nc.sbuf_tensor("vstore", [128, 32 * 1024], BF16))
    s1 = ExitStack()
    sb1 = lambda name, shape, dt: s1.enter_context(nc.sbuf_tensor(name, shape, dt))
    ps1 = [s1.enter_context(nc.psum_tensor(f"ps1_{i}", [128, 512], F32)) for i in range(8)]
    cstage = [sb1(f"cstage{i}", [128, 2240], BF16) for i in range(2)]
    xch = [sb1(f"xch{i}", [128, 2048], F32) for i in range(2)]
    ln1o = [sb1(f"ln1o{i}", [128, 2048], BF16) for i in range(2)]
    xfm = [sb1(f"xfma{i}", [128, 16 * 512], BF16) for i in range(4)]
    wqk_sb = [sb1(f"wqk_sb{i}", [128, 2048], BF16) for i in range(2)]
    wv_sb = sb1("wv_sb", [128, 16 * 512], BF16)
    qkstage = [sb1(f"qkstage{i}", [128, 512], BF16) for i in range(4)]
    b_qk_sb = sb1("b_qk_sb", [128, 16], F32)
    b_v_sb = sb1("b_v_sb", [1, 1024], F32)

    SM2 = P.qsem("sm2")
    P.dma("sp", b_qk_sb[:, :], b_qk, SM2)
    P.dma("sp", b_v_sb[:, :], b_v, SM2)
    e_small = P.n(SM2)

    CG = [P.qsem("cg0"), P.qsem("cg1")]
    CS = [P.qsem("cs0"), P.qsem("cs1")]
    cache_q = []

    def drain_cache(n):
        for _ in range(min(n, len(cache_q))):
            src, dst, w = cache_q.pop(0)
            sl = drain_cache.i % 2
            drain_cache.i += 1
            stg = cstage[sl][0:src.shape[0], 0:w]
            if len(src.shape) == 3:
                stg = stg.rearrange("p (a b) -> p a b", a=src.shape[1])
            P.dma("pool", stg, src, CG[sl], waits=[(CS[sl], P.n(CS[sl]))])
            P.dma(["sp", "act"][sl], dst, stg, CS[sl], waits=[(CG[sl], P.n(CG[sl]))])
    drain_cache.i = 0

    cache_q += [(wqkb[cc], wqkc[cc], 2048) for cc in range(16)]
    cache_q += [(wvb[dc], wvc[dc], 1024) for dc in range(16)]
    cache_q += [(woutb[hh], woutc[hh], 2048) for hh in range(8)]
    cache_q += [(w1tb[f], w1c[f], 2240) for f in range(64)]
    for do in range(4):
        cache_q += [(w2tb[do, f:f + 2].rearrange("a p b -> p a b"),
                     w2cs[do][f // 2].rearrange("p (a b) -> p a b", a=2), 1024)
                    for f in range(0, 64, 2)]

    XQ = [P.qsem("xq0"), P.qsem("xq1")]
    LW = [P.qsem("lw0"), P.qsem("lw1")]
    e_ln1w = [0] * 32
    xch_free = [0, 0]
    ln1o_free = [0, 0]

    def ln1_chunk(c):
        sl = c % 2
        P.dma(["sp", "act"][sl], xch[sl][:, :], x_batch[c * 128:(c + 1) * 128, :],
              XQ[sl], waits=[("DVE", xch_free[sl])])
        e_x = P.n(XQ[sl])
        for i in range(4):
            P.op("dve", lambda e, o=bn6[:, i * 6:(i + 1) * 6],
                 s=xch[sl][:, i * 512:(i + 1) * 512]: e.bn_stats(o, s),
                 waits=[(XQ[sl], e_x)] if i == 0 else [], sig="DVE")
        eb = P.n("DVE")
        P.op("dve", lambda e: e.bn_aggr(mv[:, :], bn6[:, :]), waits=[("DVE", eb)],
             sig="DVE")
        ea = P.n("DVE")
        P.tt(lnt[0][:, :], mv[:, 1:2], eps_sb[:, :], ALU.add,
             waits=[("DVE", ea), ("GP", e_const), ("ACT", P.n("ACT"))])
        ev = P.n("DVE")
        el = P.act(lnt[1][:, :], lnt[0][:, :], AF.Ln, waits=[("DVE", ev)])
        er = P.act(rstd[:, :], lnt[1][:, :], AF.Exp, scale=-0.5, waits=[("ACT", el)])
        P.ts(ln1o[sl][:, :], xch[sl][:, :], mv[:, 0:1], rstd[:, :],
             ALU.subtract, ALU.mult, waits=[("ACT", er), (LW[sl], ln1o_free[sl])])
        e_ln = P.n("DVE")
        xch_free[sl] = e_ln
        P.dma("sp", ln1_tm[c * 128:(c + 1) * 128, :], ln1o[sl][:, :], LW[sl],
              waits=[("DVE", e_ln)])
        e_ln1w[c] = (sl, P.n(LW[sl]))
        ln1o_free[sl] = e_ln1w[c][1]

    for c in range(16):
        ln1_chunk(c)
    drain_cache(32)
    e_wqkc = (P.n(CS[0]), P.n(CS[1]))

    TRX = [P.qsem(f"trx{i}") for i in range(4)]
    WQQ = [P.qsem("wqq0"), P.qsem("wqq1")]
    KSQ = [P.qsem(f"ksq{i}") for i in range(4)]
    WVQ = P.qsem("wvq")
    e_kqw = {}
    e_xfm = {}
    bank_free = [(None, 0)] * 8
    stage_free = [0] * 4
    wqk_free = [0, 0]
    grp_pe_end = [0] * 4
    e_vmm_prev = 0
    for grp in range(4):
        if grp < 2:
            for c in range(16 + grp * 8, 24 + grp * 8):
                ln1_chunk(c)
        for j in range(2):
            tb = grp * 2 + j
            for dc in range(16):
                w = []
                if dc == 0:
                    w = [(LW[sl_], v_) for (sl_, v_) in
                         (e_ln1w[tb * 4 + i] for i in range(4))]
                    w.append(("PE", grp_pe_end[grp - 2] if grp >= 2 else 0))
                P.dmat(xfm[tb % 4][:, dc * 512:(dc + 1) * 512],
                       ln1_tm[tb * 512:(tb + 1) * 512, dc * 128:(dc + 1) * 128],
                       TRX[tb % 4], waits=w)
            e_xfm[tb] = P.n(TRX[tb % 4])

        for cc in range(16):
            drain_cache(2)
            sl = cc % 2
            P.dma(["sp", "act"][sl], wqk_sb[sl][:, :], wqkc[cc], WQQ[sl],
                  waits=[(CS[0], e_wqkc[0]), (CS[1], e_wqkc[1]),
                         ("PE", wqk_free[sl])])
            e_w = P.n(WQQ[sl])
            bsel = [(cc % 4) * 2, (cc % 4) * 2 + 1]
            for dc in range(16):
                for j in range(2):
                    tb = grp * 2 + j
                    w = []
                    if dc == 0:
                        w = [(WQQ[sl], e_w), (TRX[tb % 4], e_xfm[tb]), bank_free[bsel[j]]]
                    P.mm(ps1[bsel[j]][:, :], wqk_sb[sl][:, dc * 128:(dc + 1) * 128],
                         xfm[tb % 4][:, dc * 512:(dc + 1) * 512],
                         start=(dc == 0), stop=(dc == 15), waits=w)
            e_qk = P.n("PE")
            wqk_free[sl] = e_qk
            h = cc // 2
            is_q = (cc % 2 == 0)
            evs = []
            for j in range(2):
                ssl = (cc * 2 + j) % 4
                evs.append(P.act(qkstage[ssl][:, :], ps1[bsel[j]][:, :], AF.Identity,
                                 bias=b_qk_sb[:, cc:cc + 1],
                                 waits=[("PE", e_qk), (SM2, e_small),
                                        (KSQ[ssl], stage_free[ssl])]))
            for j in range(2):
                tb = grp * 2 + j
                ssl = (cc * 2 + j) % 4
                bank_free[bsel[j]] = ("ACT", evs[j])
                dst = (q_buf if is_q else k_buf)[h, :, tb * 512:(tb + 1) * 512]
                P.dma(["sp", "sp", "act", "act"][ssl], dst, qkstage[ssl][:, :],
                      KSQ[ssl], waits=[("ACT", evs[j])])
                e_kqw[(h, tb, is_q)] = (ssl, P.n(KSQ[ssl]))
                stage_free[ssl] = P.n(KSQ[ssl])

        for vb in range(2):
            for dc in range(16):
                drain_cache(1)
                P.dma(["sp", "act"][dc % 2], wv_sb[:, dc * 512:(dc + 1) * 512],
                      wvc[dc][:, vb * 512:(vb + 1) * 512], WVQ,
                      waits=[(CS[0], e_wqkc[0]), (CS[1], e_wqkc[1]),
                             ("PE", e_vmm_prev)] if dc < 2 else [])
            e_wv = P.n(WVQ)
            for tk in range(8):
                j = tk // 4
                tb = grp * 2 + j
                kt_tile = grp * 8 + tk
                bank = 4 + tk % 4
                for dc in range(16):
                    w = []
                    if dc == 0:
                        w = [(WVQ, e_wv), (TRX[tb % 4], e_xfm[tb]), bank_free[bank]]
                    P.mm(ps1[bank][:, :],
                         xfm[tb % 4][:, dc * 512 + (tk % 4) * 128: dc * 512 + (tk % 4) * 128 + 128],
                         wv_sb[:, dc * 512:(dc + 1) * 512],
                         start=(dc == 0), stop=False, waits=w)
                P.mm(ps1[bank][:, :], ones1f[:, :], b_v_sb[:, vb * 512:(vb + 1) * 512],
                     start=False, stop=True, waits=[(SM2, e_small), ("GP", e_const)])
                e_pv = P.n("PE")
                ev = P.act(vstore[:, kt_tile * 1024 + vb * 512: kt_tile * 1024 + vb * 512 + 512],
                           ps1[bank][:, :], AF.Copy, bias=0.0, waits=[("PE", e_pv)])
                bank_free[bank] = ("ACT", ev)
            e_vmm_prev = P.n("PE")
        grp_pe_end[grp] = P.n("PE")

    drain_cache(10 ** 9)
    e_wc = (P.n(CS[0]), P.n(CS[1]))
    e_vdone = P.n("ACT")
    s1.close()

    s2 = ExitStack()
    sb2 = lambda name, shape, dt: s2.enter_context(nc.sbuf_tensor(name, shape, dt))
    psS = [s2.enter_context(nc.psum_tensor(f"psS{i}", [128, 512], F32)) for i in range(2)]
    psCs = [s2.enter_context(nc.psum_tensor(f"psC{i}", [128, 512], F32)) for i in range(2)]
    psN = s2.enter_context(nc.psum_tensor("psN", [1, 512], F32))
    psBC = s2.enter_context(nc.psum_tensor("psBC", [128, 512], F32))
    psO = [s2.enter_context(nc.psum_tensor(f"psO{i}", [128, 512], F32)) for i in range(2)]
    psNs = [psN[0:1, :], psO[0][0:1, :]]
    diag1 = sb2("diag1", [128, 896], BF16)
    q_sb = [sb2(f"q_sb{i}", [128, 512], BF16) for i in range(2)]
    k_sb = [sb2(f"k_sb{i}", [128, 4096], BF16) for i in range(2)]
    probsE = [sb2(f"probsE{i}", [128, 512], BF16) for i in range(4)]
    probsM = [sb2(f"probsM{i}", [128, 512], BF16) for i in range(2)]
    ctxs = sb2("ctxs", [128, 8 * 512], BF16)
    nbc_sb = sb2("nbc_sb", [128, 512], F32)
    rcp_sb = sb2("rcp_sb", [1, 512], F32)
    wout_sb = sb2("wout_sb", [128, 8 * 2048], BF16)
    ostage = [sb2(f"ostage{i}", [128, 512], BF16) for i in range(2)]
    b_out_sb = sb2("b_out_sb", [1, 2048], F32)
    b_out_h = sb2("b_out_h", [1, 2048], F32)
    x2_sb = sb2("x2_sb", [128, 2048], F32)
    ln2_sb = sb2("ln2_sb", [128, 2048], BF16)

    BOQ = P.qsem("boq")
    P.dma("sp", b_out_sb[:, :], b_out, BOQ, waits=[("ACT", e_vdone)])
    e_bo = P.n(BOQ)
    e_bouth = P.act(b_out_h[:, :], b_out_sb[:, :], AF.Copy, bias=0.0, scale=0.5,
                    waits=[(BOQ, e_bo)])

    e0 = pl(lambda g: g.memset(diag1[:, :], 1.0),
            [("ACT", e_vdone), (CS[0], e_wc[0]), (CS[1], e_wc[1])])
    e_diag = pl(lambda g: g.affine_select(diag1[:, :], diag1[:, :], pattern=[[1, 896]],
                                          compare_op=ALU.is_ge, fill=0.0, base=-384,
                                          channel_multiplier=-1), [("GP", e0)])

    WOQ = P.qsem("woq")
    for hh in range(HPC):
        P.dma("sp", wout_sb[:, hh * 2048:(hh + 1) * 2048], woutc[hh], WOQ,
              waits=[(CS[0], e_wc[0]), (CS[1], e_wc[1]), ("ACT", e_vdone)]
              if hh == 0 else [])
    e_wout = P.n(WOQ)

    QL = [P.qsem("ql0"), P.qsem("ql1")]
    KL = [P.qsem("kl0"), P.qsem("kl1")]
    OSQ = [P.qsem("osq0"), P.qsem("osq1")]
    PQA = P.qsem("pqa")
    PQB = P.qsem("pqb")

    scoresE = [0, 0]
    probsE_free = [(None, 0)] * 4
    probsM_free = [0] * 2
    q_free = [0, 0]
    k_free = [0, 0]
    ctx_free2 = [0, 0]
    n_free2 = [0, 0]
    nbc_evac = 0
    ctxs_free = 0
    o_free = [0, 0]
    psO_free = [0, 0]
    e_cc = [0] * 8
    e_x2w = [0] * 16
    e_ln2w = [0] * 16
    p3a_done = 0
    pending_tail = [None]

    def p3a_chunk(c):
        g2 = c // 2
        P.dma("pool", x2_sb[:, :], x_local[c * 128:(c + 1) * 128, :], PQA,
              waits=[("CC", e_cc[g2]), (PQA, e_x2w[c - 1] if c else 0),
                     ("DVE", P.n("DVE"))])
        eg1 = P.n(PQA)
        P.dma("pool", x2_sb[:, :], rs_out[c * 128:(c + 1) * 128, :], PQA,
              waits=[(PQA, eg1)], accum_op=ALU.add)
        e_l = P.n(PQA)
        P.dma("pool", x2_buf[c * 128:(c + 1) * 128, :], x2_sb[:, :], PQA,
              waits=[(PQA, e_l)])
        e_x2w[c] = P.n(PQA)
        for i in range(4):
            P.op("dve", lambda e, o=bn6[:, i * 6:(i + 1) * 6],
                 s=x2_sb[:, i * 512:(i + 1) * 512]: e.bn_stats(o, s),
                 waits=[(PQA, e_l)] if i == 0 else [], sig="DVE")
        eb = P.n("DVE")
        P.op("dve", lambda e: e.bn_aggr(mv[:, :], bn6[:, :]), waits=[("DVE", eb)],
             sig="DVE")
        ea = P.n("DVE")
        P.tt(lnt[0][:, :], mv[:, 1:2], eps_sb[:, :], ALU.add,
             waits=[("DVE", ea), ("ACT", P.n("ACT"))])
        ev = P.n("DVE")
        el = P.act(lnt[1][:, :], lnt[0][:, :], AF.Ln, waits=[("DVE", ev)])
        er = P.act(rstd[:, :], lnt[1][:, :], AF.Exp, scale=-0.5, waits=[("ACT", el)])
        P.ts(ln2_sb[:, :], x2_sb[:, :], mv[:, 0:1], rstd[:, :], ALU.subtract, ALU.mult,
             waits=[("ACT", er), (PQB, e_ln2w[c - 1] if c else 0)])
        e_ts = P.n("DVE")
        P.dma("pool", ln2_tm[c * 128:(c + 1) * 128, :], ln2_sb[:, :], PQB,
              waits=[("DVE", e_ts)])
        e_ln2w[c] = P.n(PQB)

    def emit_attn_tail():
        nonlocal nbc_evac
        if pending_tail[0] is None:
            return
        h_, csl_, nsl_ = pending_tail[0]
        pending_tail[0] = None
        e_bc = P.mm(psBC[:, :], ones1f[:, :], rcp_sb[:, :], start=True, stop=True,
                    waits=[("DVE", n_free2[nsl_]), ("DVE", nbc_evac)])
        P.op("dve", lambda e: e.tensor_copy(nbc_sb[:, :], psBC[:, :]),
             waits=[("PE", e_bc)], sig="DVE")
        nbc_evac = P.n("DVE")
        P.tt(ctxs[:, h_ * 512:(h_ + 1) * 512], psCs[csl_][:, :], nbc_sb[:, :],
             ALU.mult, waits=[("DVE", nbc_evac), ("PE", ctxs_free)])
        ctx_free2[csl_] = P.n("DVE")

    for g in range(NG):
        ext = 4 * (g + 1)
        for hi in range(HPC):
            hh_idx = g * HPC + hi
            qsl = hh_idx % 2
            csl = hh_idx % 2
            nsl = hi % 2
            ssl_q, val_q = e_kqw[(hi, g, True)]
            P.dma("sp", q_sb[qsl][:, :], q_buf[hi, :, g * 512:(g + 1) * 512], QL[qsl],
                  waits=[(KSQ[ssl_q], val_q), ("PE", q_free[qsl])])
            e_q = P.n(QL[qsl])
            kw = [(KSQ[i], max((v for (sl_, v) in
                                (e_kqw[(hi, tb, False)] for tb in range(g + 1))
                                if sl_ == i), default=0)) for i in range(4)]
            P.dma("sp", k_sb[qsl][:, 0:ext * 128], k_buf[hi, :, 0:ext * 128], KL[qsl],
                  waits=kw + [("PE", k_free[qsl])])
            e_k = P.n(KL[qsl])

            def sc_step(i):
                bs = i % 2
                w = [(QL[qsl], e_q), (KL[qsl], e_k), ("ACT", scoresE[bs])]
                if g == 0 and hi == 0 and i == 0:
                    w.append(("ACT", e_vdone))
                e_s = P.mm(psS[bs][:, :], k_sb[qsl][:, i * 128:(i + 1) * 128],
                           q_sb[qsl][:, :], start=True, stop=True, waits=w)
                pe_sl = i % 4
                scoresE[bs] = P.act(probsE[pe_sl][:, :], psS[bs][:, :], AF.Exp,
                                    bias=csh_sb[:, 0:1], scale=ISQ,
                                    waits=[("PE", e_s), probsE_free[pe_sl],
                                           ("GP", e_const)])
                e_e = scoresE[bs]
                diag_m = i - 4 * g
                if diag_m >= 0:
                    pm_sl = diag_m % 2
                    off = (3 - diag_m) * 128
                    P.tt(probsM[pm_sl][:, :], probsE[pe_sl][:, :],
                         diag1[:, off:off + 512], ALU.mult,
                         waits=[("ACT", e_e), ("GP", e_diag),
                                ("PE", probsM_free[pm_sl])])
                    e_tt = P.n("DVE")
                    return (i, pe_sl, pm_sl, probsM[pm_sl], [("DVE", e_tt)], e_tt)
                return (i, pe_sl, None, probsE[pe_sl], [("ACT", e_e)], None)

            def npv_step(st):
                i, pe_sl, pm_sl, pt, w_pe, e_tt = st
                w = list(w_pe)
                if i == 0:
                    w += [("DVE", n_free2[nsl]), ("DVE", ctx_free2[csl])]
                    if nsl == 1:
                        w.append(("ACT", psO_free[0]))
                P.mm(psNs[nsl], ones128[:, :], pt[:, :], start=(i == 0),
                     stop=(i == ext - 1), waits=w)
                lp = P.mm(psCs[csl][:, :],
                          vstore[:, i * 1024 + hi * 128: i * 1024 + hi * 128 + 128],
                          pt[:, :], start=(i == 0), stop=(i == ext - 1))
                probsE_free[pe_sl] = (("DVE", e_tt) if pm_sl is not None
                                      else ("PE", lp))
                if pm_sl is not None:
                    probsM_free[pm_sl] = lp
                return lp

            prev = sc_step(0)
            emit_attn_tail()
            last_pv = 0
            for i in range(1, ext):
                st = sc_step(i)
                last_pv = npv_step(prev)
                prev = st
            last_pv = npv_step(prev)
            q_free[qsl] = last_pv
            k_free[qsl] = last_pv
            P.op("dve", lambda e, nb=psNs[nsl]: e.reciprocal(rcp_sb[:, :], nb),
                 waits=[("PE", last_pv), ("DVE", P.n("DVE"))], sig="DVE")
            n_free2[nsl] = P.n("DVE")
            pending_tail[0] = (hi, csl, nsl)
        emit_attn_tail()
        e_ctx_all = P.n("DVE")

        for tc in range(4):
            for do in range(4):
                bank = (tc * 4 + do) % 2
                w = [("DVE", e_ctx_all), (WOQ, e_wout), ("ACT", e_bouth),
                     ("ACT", psO_free[bank])]
                if tc == 0 and do == 0:
                    w.append(("DVE", n_free2[1]))
                for hh in range(HPC):
                    P.mm(psO[bank][:, :],
                         ctxs[:, hh * 512 + tc * 128: hh * 512 + tc * 128 + 128],
                         wout_sb[:, hh * 2048 + do * 512: hh * 2048 + do * 512 + 512],
                         start=(hh == 0), stop=False, waits=w if hh == 0 else [])
                e_o = P.mm(psO[bank][:, :], ones1f[:, :],
                           b_out_h[0:1, do * 512:(do + 1) * 512], start=False, stop=True)
                osl = (tc * 4 + do) % 2
                psO_free[bank] = P.act(ostage[osl][:, :], psO[bank][:, :], AF.Copy,
                                       bias=0.0,
                                       waits=[("PE", e_o), (OSQ[osl], o_free[osl])])
                P.dma("sp", rs_in[g * 512 + tc * 128: g * 512 + tc * 128 + 128,
                                  do * 512:(do + 1) * 512],
                      ostage[osl][:, :], OSQ[osl], waits=[("ACT", psO_free[bank])])
                o_free[osl] = P.n(OSQ[osl])
        ctxs_free = P.n("PE")
        e_rsw0 = P.n(OSQ[0])
        e_rsw1 = P.n(OSQ[1])

        P.op("pool", lambda gp, a=rs_in[g * 512:(g + 1) * 512, :].opt(),
             b=rs_out[g * 256:(g + 1) * 256, :].opt():
             gp.collective_compute("ReduceScatter", ALU.add,
                                   replica_groups=[[0, 1], [2, 3], [4, 5], [6, 7]],
                                   ins=[a], outs=[b]),
             waits=[(OSQ[0], e_rsw0), (OSQ[1], e_rsw1)], sig="CC")
        e_cc[g] = P.n("CC")
        while p3a_done < 2 * (g - 1):
            p3a_chunk(p3a_done)
            p3a_done += 1
    while p3a_done < 16:
        p3a_chunk(p3a_done)
        p3a_done += 1

    e_p2pe = P.n("PE")
    e_p2act = P.n("ACT")
    e_p2dve = P.n("DVE")
    s2.close()
    s12.close()

    s3 = ExitStack()
    sb3 = lambda name, shape, dt: s3.enter_context(nc.sbuf_tensor(name, shape, dt))
    ps3 = [s3.enter_context(nc.psum_tensor(f"ps3_{i}", [128, 512], F32)) for i in range(6)]
    xfm2 = sb3("xfm2", [128, 16 * 512], BF16)
    w1buf = [sb3(f"w1buf{i}", [128, 2240], BF16) for i in range(3)]
    w2buf = [sb3(f"w2buf{i}", [128, 1024], BF16) for i in range(3)]
    h_sb = [sb3("h0_sb", [128, 64 * 512], BF16), sb3("h1_sb", [128, 64 * 512], BF16)]
    tmpb1 = sb3("tmpb1", [128, 512], F32)
    gin = sb3("gin", [128, 512], F32)
    x2t = [sb3(f"x2t{i}", [128, 512], F32) for i in range(2)]
    ot = [sb3(f"ot{i}", [128, 512], F32) for i in range(2)]
    ot2 = [sb3(f"ot2_{i}", [128, 512], F32) for i in range(2)]
    b1_sb = sb3("b1_sb", [128, 64], F32)
    b2_sb = sb3("b2_sb", [1, 2048], F32)
    br_sb = sb3("br_sb", [1, 8], F32)
    oh_fm = [sb3(f"oh_fm{k}", [8, 512], BF16) for k in range(2)]
    m1_sb = [sb3(f"m1_sb{k}", [64, 512], BF16) for k in range(2)]
    z1_sb = [sb3(f"z1_sb{k}", [64, 512], BF16) for k in range(2)]
    z2_sb = [sb3(f"z2_sb{k}", [64, 512], BF16) for k in range(2)]

    e_join = [("PE", e_p2pe), ("ACT", e_p2act), ("DVE", e_p2dve)]
    SM3 = P.qsem("sm3")
    P.dma("sp", b1_sb[:, :], b1h, SM3, waits=[("ACT", e_p2act), ("DVE", e_p2dve),
                                              ("PE", e_p2pe)])
    P.dma("sp", b2_sb[:, :], b2h, SM3)
    P.dma("sp", br_sb[:, :], brh, SM3)
    e_small3 = P.n(SM3)

    TR2 = P.qsem("tr2")
    W1S = [P.qsem(f"w1s{i}") for i in range(3)]
    W1Q = ["sp", "pool", "act"]
    W2S = [P.qsem(f"w2s{i}") for i in range(3)]
    W2Q = ["act", "sp", "pool"]
    X2TQ = [P.qsem("x2tq0"), P.qsem("x2tq1")]
    OUTQ = [P.qsem("outq0"), P.qsem("outq1")]

    xfm2_free = 0
    w1_free = [0] * 3
    w2_free = [0] * 3
    e_tail_dve = 0
    x2t_free = [0, 0]
    ot_free = [0, 0]
    ot2_free = [0, 0]

    for t in range(4):
        for dc in range(16):
            w = []
            if dc == 0:
                w = [(PQB, e_ln2w[4 * t + 3]), ("PE", xfm2_free)]
                if t == 0:
                    w += e_join
            P.dmat(xfm2[:, dc * 512:(dc + 1) * 512],
                   ln2_tm[t * 512:(t + 1) * 512, dc * 128:(dc + 1) * 128], TR2,
                   waits=w)
        e_xf = P.n(TR2)

        sps = ExitStack()
        psB16 = sps.enter_context(nc.psum_tensor(f"psB16_{t}", [8, 128], BF16))
        e_m1 = [0, 0]
        e_exp_prev = 0
        for tc in range(4):
            w = [(TR2, e_xf), (SMQ, e_smallw), ("DVE", e_tail_dve),
                 ("ACT", e_exp_prev)]
            if t == 0 and tc == 0:
                w += e_join
            for dc in range(16):
                P.mm(ps3[4][0:128, 0:8],
                     xfm2[:, dc * 512 + tc * 128: dc * 512 + tc * 128 + 128],
                     wrt_sb[:, dc * 8:(dc + 1) * 8], start=(dc == 0), stop=False,
                     waits=w if dc == 0 else [])
            e_r = P.mm(ps3[4][0:128, 0:8], ones1f[:, :], br_sb[:, :], start=False,
                       stop=True, waits=[(SM3, e_small3)])
            P.op("dve", lambda e: e.reduce_max(r_sc[0][:, :], ps3[4][0:128, 0:8],
                                               axis=AX.X, negate=True),
                 waits=[("PE", e_r), ("DVE", P.n("DVE"))], sig="DVE")
            e1 = P.n("DVE")
            e2 = P.act(probs_r[:, :], ps3[4][0:128, 0:8], AF.Exp, bias=r_sc[0][:, :],
                       waits=[("DVE", e1), ("ACT", P.n("ACT"))])
            e_exp_prev = e2
            P.op("dve", lambda e: e.reduce_sum(r_sc[1][:, :], probs_r[:, :], axis=AX.X),
                 waits=[("ACT", e2)], sig="DVE")
            e3 = P.n("DVE")
            P.op("dve", lambda e: e.reciprocal(r_sc[2][:, :], r_sc[1][:, :]),
                 waits=[("DVE", e3)], sig="DVE")
            e4 = P.n("DVE")
            P.ts(probs_n[:, :], probs_r[:, :], r_sc[2][:, :], None, ALU.mult, ALU.bypass,
                 waits=[("DVE", e4)])
            e5 = P.n("DVE")
            P.op("dve", lambda e, o=w0_st[:, tc:tc + 1]:
                 e.reduce_max(o, probs_n[:, :], axis=AX.X), waits=[("DVE", e5)],
                 sig="DVE")
            e6 = P.n("DVE")
            P.ts(oh_tm[0][:, :], probs_n[:, :], w0_st[:, tc:tc + 1], None,
                 ALU.is_equal, ALU.bypass, waits=[("DVE", e6), ("PE", P.n("PE"))])
            e7 = P.n("DVE")
            P.tt(probs_r[:, :], probs_n[:, :], oh_tm[0][:, :], ALU.mult,
                 waits=[("DVE", e7)])
            e7b = P.n("DVE")
            P.tt(probs_m[:, :], probs_n[:, :], probs_r[:, :], ALU.subtract,
                 waits=[("DVE", e7b)])
            e8 = P.n("DVE")
            P.op("dve", lambda e, o=w1_st[:, tc:tc + 1]:
                 e.reduce_max(o, probs_m[:, :], axis=AX.X), waits=[("DVE", e8)],
                 sig="DVE")
            e9 = P.n("DVE")
            P.ts(oh_tm[1][:, :], probs_m[:, :], w1_st[:, tc:tc + 1], None,
                 ALU.is_equal, ALU.bypass, waits=[("DVE", e9), ("PE", P.n("PE"))])
            e10 = P.n("DVE")
            for k in range(2):
                ep = P.transp(psB16[0:8, 0:128], oh_tm[k][:, :], ident[:, :],
                              waits=[("DVE", e10), ("GP", e_ident),
                                     ("ACT", P.n("ACT")), ("DVE", e_tail_dve)])
                P.act(oh_fm[k][:, tc * 128:(tc + 1) * 128], psB16[0:8, 0:128], AF.Copy,
                      bias=0.0, waits=[("PE", ep)])
        e_oh = P.n("ACT")
        for k in range(2):
            em = P.mm(ps3[5][0:64, :], rep2[:, :], oh_fm[k][:, :], start=True, stop=True,
                      waits=[("ACT", e_oh), ("GP", e_rep2), ("DVE", P.n("DVE"))])
            P.op("dve", lambda e, k=k: e.tensor_copy(m1_sb[k][:, :], ps3[5][0:64, :]),
                 waits=[("PE", em)], sig="DVE")
            e_m1[k] = P.n("DVE")
        sps.close()
        psE = ExitStack()
        ps6 = psE.enter_context(nc.psum_tensor(f"ps3_6_{t}", [128, 512], F32))
        ps7 = psE.enter_context(nc.psum_tensor(f"ps3_7_{t}", [128, 512], F32))
        ps8 = [ps3[0], ps3[1], ps3[2], ps3[3], ps3[4], ps3[5], ps6, ps7]

        for dc in range(16):
            P.mm(ps3[5][0:64, :], a1t_sb[:, dc * 64:(dc + 1) * 64],
                 xfm2[:, dc * 512:(dc + 1) * 512], start=(dc == 0), stop=(dc == 15),
                 waits=[(SMQ, e_smallw), ("DVE", e_m1[1])] if dc == 0 else [])
        e_u1 = P.n("PE")
        ez1 = []
        for k in range(2):
            P.tt(z1_sb[k][:, :], ps3[5][0:64, :], m1_sb[k][:, :], ALU.mult,
                 waits=[("PE", e_u1), ("DVE", e_m1[k])])
            ez1.append(P.n("DVE"))

        tmp_free = 0
        gin_free = 0
        e_b1evac = {}
        e_gelu = {}
        e_ttg = {}
        e_u2mm = [0, 0]
        for f in range(64):
            wsl = f % 3
            P.dma(W1Q[wsl], w1buf[wsl][:, :], w1c[f], W1S[wsl],
                  waits=[(CS[0], e_wc[0]), (CS[1], e_wc[1]), ("PE", w1_free[wsl])] +
                        (e_join if (t == 0 and f < 3) else []))
            e_wl = P.n(W1S[wsl])
            bb = f % 2
            for dc in range(16):
                w = []
                if dc == 0:
                    w = [(W1S[wsl], e_wl), (TR2, e_xf), ("ACT", e_b1evac.get(f - 2, 0))]
                    if f == 0:
                        w.append(("DVE", e_m1[1]))
                P.mm(ps3[bb][:, :], w1buf[wsl][:, dc * 128:(dc + 1) * 128],
                     xfm2[:, dc * 512:(dc + 1) * 512], start=(dc == 0), stop=(dc == 15),
                     waits=w)
            e_b1 = P.n("PE")
            e_b1evac[f] = P.act(tmpb1[:, :], ps3[bb][:, :], AF.Identity,
                                bias=b1_sb[:, f:f + 1],
                                waits=[("PE", e_b1), ("DVE", tmp_free),
                                       (SM3, e_small3)])
            for k in range(2):
                bz = 2 + k
                e_zb = P.mm(ps3[bz][:, :], w1buf[wsl][0:64, 2048:2176], z1_sb[k][:, :],
                            start=True, stop=True,
                            waits=[("DVE", ez1[k]), ("DVE", e_ttg.get((f - 1, k), 0))])
                P.tt(gin[:, :], ps3[bz][:, :], tmpb1[:, :], ALU.add,
                     waits=[("PE", e_zb), ("ACT", e_b1evac[f]), ("ACT", gin_free)])
                e_ttg[(f, k)] = P.n("DVE")
                if k == 1:
                    tmp_free = e_ttg[(f, k)]
                e_gelu[(f, k)] = P.act(h_sb[k][:, f * 512:(f + 1) * 512], gin[:, :],
                                       AF.Gelu_apprx_tanh,
                                       waits=[("DVE", e_ttg[(f, k)])])
                gin_free = e_gelu[(f, k)]
                e_u2mm[k] = P.mm(ps3[4 + k][0:64, :], w1buf[wsl][:, 2176:2240],
                                 h_sb[k][:, f * 512:(f + 1) * 512],
                                 start=(f == 0), stop=(f == 63),
                                 waits=[("ACT", e_gelu[(f, k)])])
            w1_free[wsl] = P.n("PE")
        ez2 = []
        for k in range(2):
            P.tt(z2_sb[k][:, :], ps3[4 + k][0:64, :], m1_sb[k][:, :], ALU.mult,
                 waits=[("PE", e_u2mm[k]), ("DVE", e_m1[k])])
            ez2.append(P.n("DVE"))
        e_b1last = e_b1evac[63]
        e_ttglast = e_ttg[(63, 1)]

        e_stt2 = {}
        for do in range(4):
            for tc in range(4):
                for k in range(2):
                    bank = ps8[k * 4 + tc]
                    w = [(SM3, e_small3), ("DVE", ez2[k])]
                    if do == 0:
                        w += [("ACT", e_b1last), ("DVE", e_ttglast), ("DVE", ez2[1]),
                              ("DVE", e_tail_dve)]
                    else:
                        w.append(("DVE", e_stt2[(do - 1, tc)]))
                    P.mm(bank[:, :], ones1f[:, :], b2_sb[0:1, do * 512:(do + 1) * 512],
                         start=True, stop=False, waits=w)
                    P.mm(bank[:, :], z2_sb[k][:, tc * 128:(tc + 1) * 128],
                         b2rf_sb[0:64, do * 512:(do + 1) * 512], start=False, stop=False)
            for fq in range(32):
                wsl = fq % 3
                P.dma(W2Q[wsl], w2buf[wsl][:, :], w2cs[do][fq], W2S[wsl],
                      waits=[(CS[0], e_wc[0]), (CS[1], e_wc[1]),
                             ("PE", w2_free[wsl])] +
                            (e_join if (t == 0 and do == 0 and fq < 3) else []))
                e_w2 = P.n(W2S[wsl])
                for fi in range(2):
                    f = 2 * fq + fi
                    for k in range(2):
                        for tc in range(4):
                            w = []
                            if fi == 0 and k == 0 and tc == 0:
                                w = [(W2S[wsl], e_w2), ("ACT", e_gelu[(63, 1)])]
                            P.mm(ps8[k * 4 + tc][:, :],
                                 h_sb[k][:, f * 512 + tc * 128: f * 512 + tc * 128 + 128],
                                 w2buf[wsl][:, fi * 512:(fi + 1) * 512],
                                 start=False, stop=(f == 63), waits=w)
                w2_free[wsl] = P.n("PE")
            e_stop = P.n("PE")

            for tc in range(4):
                xsl = tc % 2
                P.dma("sp", x2t[xsl][:, :],
                      x2_buf[(t * 4 + tc) * 128:(t * 4 + tc) * 128 + 128,
                             do * 512:(do + 1) * 512], X2TQ[xsl],
                      waits=[(PQA, e_x2w[t * 4 + tc]), ("DVE", x2t_free[xsl])])
                e_xl = P.n(X2TQ[xsl])
                osl = (do * 4 + tc) % 2
                P.stt(ot[osl][:, :], ps8[tc][:, :], w0_st[:, tc:tc + 1],
                      x2t[xsl][:, :], ALU.mult, ALU.add,
                      waits=[("PE", e_stop), (X2TQ[xsl], e_xl), ("DVE", ot_free[osl])])
                x2t_free[xsl] = P.n("DVE")
                P.stt(ot2[osl][:, :], ps8[4 + tc][:, :], w1_st[:, tc:tc + 1],
                      ot[osl][:, :], ALU.mult, ALU.add,
                      waits=[("DVE", x2t_free[xsl]), (OUTQ[osl], ot2_free[osl])])
                e_stt2[(do, tc)] = P.n("DVE")
                ot_free[osl] = e_stt2[(do, tc)]
                P.dma("sp", out_t[(t * 4 + tc) * 128:(t * 4 + tc) * 128 + 128,
                                  do * 512:(do + 1) * 512],
                      ot2[osl][:, :], OUTQ[osl], waits=[("DVE", e_stt2[(do, tc)])])
                ot2_free[osl] = P.n(OUTQ[osl])
        e_tail_dve = P.n("DVE")
        xfm2_free = P.n("PE")
        psE.close()

    if taps:
        TAPQ = P.qsem("tapq")
        allq = [(name, P.n(name)) for name in P.sems if name not in ("CC",)]
        for tname in taps:
            src = tap_src[tname][0]
            P.dma("pool", tap_aps[tname], src[tuple(slice(None) for _ in src.shape)],
                  TAPQ, waits=allq + [("CC", P.n("CC"))])
            allq = []
        P.op("sp", lambda e: e.nop(),
             waits=[(TAPQ, P.n(TAPQ)), (OUTQ[0], P.n(OUTQ[0])),
                    (OUTQ[1], P.n(OUTQ[1]))], sig=None)
    else:
        P.op("sp", lambda e: e.nop(), waits=[(OUTQ[0], P.n(OUTQ[0])),
                                             (OUTQ[1], P.n(OUTQ[1]))], sig=None)

    s3.close()
    with nc.Block() as block:
        P.play(block)
    es.close()
    return nc


def _local_rows(r):
    return np.concatenate([np.arange(g * 512 + r * 256, g * 512 + r * 256 + 256)
                           for g in range(8)])


def _prep_core_inputs(c, I):
    b, r = c // 2, c % 2
    A = lambda x: np.ascontiguousarray(x, dtype=np.float32)
    X = np.asarray(I["L0_attn_x"], dtype=np.float32)
    w_qkv = np.asarray(I["L0_attn_w_qkv"], dtype=np.float32)
    heads = range(r * 8, r * 8 + 8)
    qk_cols = np.concatenate([np.r_[h * 384:h * 384 + 128, h * 384 + 128:h * 384 + 256]
                              for h in heads])
    v_cols = np.concatenate([np.r_[h * 384 + 256:h * 384 + 384] for h in heads])
    w_qk = w_qkv[:, qk_cols]
    d = {}
    d["x_batch"] = A(X[b])
    d["x_local"] = A(X[b][_local_rows(r)])
    d["wqkb"] = A(w_qk.reshape(16, 128, 16, 128).transpose(2, 1, 0, 3).reshape(16, 128, 2048))
    b_qkv = np.asarray(I["L0_attn_b_qkv"], dtype=np.float32)
    d["b_qk"] = A(b_qkv[qk_cols].reshape(16, 128).T)
    d["wvb"] = A(w_qkv[:, v_cols].reshape(16, 128, 1024))
    d["b_v"] = A(b_qkv[v_cols].reshape(1, 1024))
    w_out = np.asarray(I["L0_attn_w_out"], dtype=np.float32)
    d["woutb"] = A(w_out[r * 1024:(r + 1) * 1024].reshape(8, 128, 2048))
    d["b_out"] = A(np.asarray(I["L0_attn_b_out"], np.float32).reshape(1, 2048))
    W1 = np.asarray(I["L0_ffn_W1"], dtype=np.float32)
    w1tb = np.zeros((64, 128, 2240), np.float32)
    w1tb[:, :, 0:2048] = W1.reshape(64, 128, 16, 128).transpose(0, 3, 2, 1).reshape(64, 128, 2048)
    B1rf = np.asarray(I["L0_ffn_B1"], np.float32).transpose(0, 2, 1).reshape(64, 8192)
    w1tb[:, 0:64, 2048:2176] = B1rf.reshape(64, 64, 128).transpose(1, 0, 2)
    A2rf = np.asarray(I["L0_ffn_A2"], np.float32).reshape(64, 8192)
    w1tb[:, :, 2176:2240] = A2rf.T.reshape(64, 128, 64)
    d["w1tb"] = A(w1tb)
    d["b1h"] = A(np.asarray(I["L0_ffn_b1"], np.float32).reshape(64, 128).T)
    W2T = np.asarray(I["L0_ffn_W2"], np.float32).T
    d["w2tb"] = A(W2T.reshape(64, 128, 4, 512).transpose(2, 0, 1, 3))
    d["b2h"] = A(np.asarray(I["L0_ffn_b2"], np.float32).reshape(1, 2048))
    A1rf = np.asarray(I["L0_ffn_A1"], np.float32).reshape(64, 2048)
    d["a1tb"] = A(A1rf.T.reshape(16, 128, 64).transpose(1, 0, 2).reshape(128, 1024))
    b2rf = np.zeros((128, 2048), np.float32)
    b2rf[0:64] = np.asarray(I["L0_ffn_B2"], np.float32).transpose(0, 2, 1).reshape(64, 2048)
    d["b2rfh"] = A(b2rf)
    d["wrtb"] = A(np.asarray(I["L0_ffn_W_r"], np.float32).T.reshape(16, 128, 8)
                  .transpose(1, 0, 2).reshape(128, 128))
    d["brh"] = A(np.asarray(I["L0_ffn_b_r"], np.float32).reshape(1, 8))
    return d


def kernel(**inputs):
    nc = build_program()
    in_maps = [_prep_core_inputs(c, inputs) for c in range(NCORES)]
    res = run_bass_kernel_spmd(nc, in_maps, core_ids=list(range(NCORES)))
    out = np.empty((B, S, D), np.float32)
    for c in range(NCORES):
        b, r = c // 2, c % 2
        out[b][_local_rows(r)] = res.results[c]["out"]
    return out
